# version 07
# baseline (speedup 1.0000x reference)
import sys
sys.path.insert(0, '/opt/trn_rl_repo')
import numpy as np
import ml_dtypes
from contextlib import ExitStack

import concourse.bass as bass
import concourse.mybir as mybir
from concourse.bass_utils import run_bass_kernel_spmd

F32 = mybir.dt.float32
BF16 = mybir.dt.bfloat16
AF = mybir.ActivationFunctionType
ALU = mybir.AluOpType
AXX = mybir.AxisListType.X

HID = 2048
NCH = 16
S = 4096
TOWN = 1024
NH = 16
NKV = 4
FF = 7168
NFC = 56
SCALE = 1.0 / float(np.sqrt(128.0))
EPS = 1e-5
TA = 256
NTA = S // TA
NTB = TOWN // TA
NT = NTA + NTB
TD = 512
KBB = 1024

ENGS = ('tensor', 'vector', 'scalar', 'sync', 'gpsimd')
SB0 = 17536
SB_WORK = SB0 + 128 * 1024
SB_GLB = 216064
SB_END = 229376


class Arena:
    def __init__(self, nc, base, cap, tag):
        self.nc, self.off, self.cap, self.tag = nc, base, cap, tag

    def al(self, name, shape, dtype):
        bytes_pp = int(np.prod(shape[1:])) * (2 if dtype == BF16 else 4)
        off = (self.off + 31) // 32 * 32
        assert off + bytes_pp <= self.cap, (self.tag, name, off, bytes_pp, self.cap)
        self.off = off + bytes_pp
        t = self.nc.alloc_sbuf_tensor_at(f'{self.tag}_{name}', list(shape), dtype, offset=off)
        return t.ap()


class _P:
    def __init__(self, sems):
        self.sems = sems
        self.streams = {k: [] for k in ENGS}
        self.cnt = {k: 0 for k in sems}

    def bump(self, sem, n=1):
        self.cnt[sem] += n
        return self.cnt[sem]

    def dma(self, q, dst, src, sem=None):
        assert sem is not None
        v = self.bump(sem, 16)
        s = self.sems[sem]

        def run(e, dst=dst, src=src, s=s):
            e.dma_start(out=dst, in_=src).then_inc(s, 16)
        self.streams[q].append(run)
        return v

    def wait(self, eng, sem, v):
        if not v:
            return
        s = self.sems[sem]
        self.streams[eng].append(lambda e, s=s, v=v: e.wait_ge(s, v))

    def g(self, eng, builders, waits=(), inc=None, incn=1):
        ws = tuple((s, v) for (s, v) in waits if v)
        vinc = self.bump(inc, incn) if inc else None
        sems = self.sems

        def run(e, builders=tuple(builders), ws=ws, inc=inc, incn=incn):
            for sname, v in ws:
                e.wait_ge(sems[sname], v)
            last = None
            for f in builders:
                last = f(e)
            if inc is not None:
                last.then_inc(sems[inc], incn)
        self.streams[eng].append(run)
        return vinc


def build_program():
    nc = bass.Bass('TRN2', target_bir_lowering=False, debug=False)

    def din(name, shape, dtype=F32):
        return nc.dram_tensor(name, list(shape), dtype, kind='ExternalInput').ap()

    x_tiles = din('x_tiles', [NTA, 128, NCH * TA])
    xo_tiles = din('xo_tiles', [NTB, 128, NCH * TA])
    xo_oc = din('xo_oc', [NCH, 128, TOWN])
    wq_h = din('wq_h', [NH, 128, NCH * 128])
    wk_h = din('wk_h', [128, NCH * 512])
    wv_h = din('wv_h', [128, NCH * 512])
    wo_h = din('wo_h', [NCH, 128, NCH * 128])
    bq_h = din('bq_h', [128, NH])
    bk_h = din('bk_h', [128, NKV])
    bo_h = din('bo_h', [128, NCH])
    bv_h = din('bv_h', [128, 512])
    masks_h = din('masks_h', [128, 32 * 512], BF16)
    wgt_h = din('wgt_h', [128, NCH * 8])
    w1_h = din('w1_h', [NFC, 128, NCH * 128])
    w3_h = din('w3_h', [NFC, 128, NCH * 128])
    w2_h = din('w2_h', [NCH, 4, 128, 14 * 128])
    a1_h = din('a1_h', [128, NCH * 256])
    a3_h = din('a3_h', [128, NCH * 256])
    b1_h = din('b1_h', [28, 128, 2 * 256])
    b3_h = din('b3_h', [28, 128, 2 * 256])
    a2_h = din('a2_h', [14, 128, 4 * 256])
    b2_h = din('b2_h', [128, 2 * HID])

    out_d = nc.dram_tensor('out_d', [NCH, 128, 2, TD], F32, kind='ExternalOutput').ap()
    xres_d = nc.dram_tensor('xres_d', [2, 128, NCH, TD], F32, kind='Internal').ap()

    c_ocb = nc.inline_tensor(np.ones((128, 1), ml_dtypes.bfloat16), name='c_ocb').ap()
    c_ocf = nc.inline_tensor(np.ones((128, 1), np.float32), name='c_ocf').ap()
    c_orf = nc.inline_tensor(np.ones((1, 128), np.float32), name='c_orf').ap()
    c_one = nc.inline_tensor(np.ones((1, 1), np.float32), name='c_one').ap()
    c_idf = nc.inline_tensor(np.eye(128, dtype=np.float32), name='c_idf').ap()
    c_nbi = nc.inline_tensor((np.eye(128, dtype=np.float32) * -60.0).astype(ml_dtypes.bfloat16),
                             name='c_nbi').ap()
    _exp = np.zeros((8, 256), np.float32)
    for ei in range(8):
        _exp[ei, ei * 32:(ei + 1) * 32] = 1.0
    c_exp = nc.inline_tensor(_exp.astype(ml_dtypes.bfloat16), name='c_exp').ap()

    es = ExitStack()
    with es:
        PP = Arena(nc, SB0, SB_WORK, 'pp')
        K_sb = PP.al('K', [128, NKV * S], BF16)
        V_sb = PP.al('V', [128, (S // 128) * 512], BF16)
        xlnown = PP.al('xlnown', [128, NCH * TOWN], BF16)
        acat = PP.al('acat', [128, NH * TOWN], BF16)
        WKV = Arena(nc, SB0 + 96 * KBB, SB_WORK, 'wkv')
        wk_sb = WKV.al('wk', [128, NCH * 512], BF16)
        wv_sb = WKV.al('wv', [128, NCH * 512], BF16)

        WA = Arena(nc, SB_WORK, SB_GLB, 'wa')
        xb_t = [WA.al(f'xb{i}', [128, NCH * TA], BF16) for i in range(2)]
        xln_t = [WA.al(f'xln{i}', [128, NCH * TA], BF16) for i in range(2)]
        sq_t = WA.al('sq', [128, NCH * TA], BF16)
        bvr_sb = WA.al('bvr', [128, 512], F32)
        rbb_t = [WA.al(f'rbb{i}', [128, 2 * TA], BF16) for i in range(2)]
        rowsA = WA.al('rowsA', [1, 3 * TA], F32)

        WC = Arena(nc, SB_WORK, SB_GLB, 'wc')
        wq_sb = [WC.al(f'wq{i}', [128, NCH * 128], BF16) for i in range(2)]
        q_sb = [WC.al(f'q{i}', [128, TOWN], BF16) for i in range(2)]
        p_sb = [WC.al(f'p{i}', [128, TOWN], BF16) for i in range(4)]
        rl_sb = WC.al('rl', [128, TOWN], F32)
        csq = WC.al('csq', [128, TOWN], BF16)
        mk_sb = [WC.al(f'mk{i}', [128, 512], BF16) for i in range(2)]
        recA = WC.al('recA', [1, 512], F32)
        recB = WC.al('recB', [1, 512], F32)
        wo_sb = [WC.al(f'wo{i}', [128, NCH * 128], BF16) for i in range(2)]
        xoc_sb = [WC.al(f'xoc{i}', [128, TOWN], F32) for i in range(2)]
        ev_sb = [WC.al(f'ev{i}', [128, TOWN], F32) for i in range(2)]

        WD = Arena(nc, SB_WORK, SB_GLB, 'wd')
        w13 = [WD.al(f'w13_{i}', [128, 2 * NCH * 128], BF16) for i in range(2)]
        xln2 = WD.al('xln2', [128, NCH * TD], BF16)
        b13 = [WD.al(f'b13_{i}', [128, 2 * 512], BF16) for i in range(2)]
        a2s = [WD.al(f'a2s_{i}', [128, 4 * 256], BF16) for i in range(2)]
        w2s = [WD.al(f'w2s_{i}', [128, 14 * 128], BF16) for i in range(3)]
        b2_sb = WD.al('b2', [128, 2 * HID], BF16)
        rowsD = WD.al('rowsD', [1, 3 * TD], F32)

        DP = Arena(nc, SB0, SB_WORK, 'dp')
        x2s = DP.al('x2s', [128, NFC * TD], BF16)
        xrt = DP.al('xrt', [128, NCH * TD], F32)
        c1 = [DP.al(f'c1_{k}', [128, 2 * TD], BF16) for k in range(2)]
        c3 = [DP.al(f'c3_{k}', [128, 2 * TD], BF16) for k in range(2)]
        nc1 = DP.al('nc1', [128, 2 * TD], BF16)
        nc3 = DP.al('nc3', [128, 2 * TD], BF16)
        mer = [DP.al(f'mer{k}', [128, 2 * TD], BF16) for k in range(2)]
        wmer = [DP.al(f'wmer{k}', [128, 2 * TD], BF16) for k in range(2)]
        s_scr = [[DP.al(f's{k}_{par}', [128, TD], BF16) for par in range(2)] for k in range(2)]
        x2_scr = [[DP.al(f'x2_{k}_{par}', [128, TD], BF16) for par in range(2)] for k in range(2)]
        d_scr = [DP.al(f'dd{par}', [128, TD], BF16) for par in range(2)]
        rwb0 = DP.al('rwb0', [128, TD], BF16)
        rbb2 = DP.al('rbb2', [128, 2 * TD], BF16)
        xb2a = x2s[:, 0:NCH * TD]
        sqda = x2s[:, NCH * TD:2 * NCH * TD]

        GLB = Arena(nc, SB_GLB, SB_END, 'gl')
        ocb_sb = GLB.al('ocb', [128, 1], BF16)
        ocf_sb = GLB.al('ocf', [128, 1], F32)
        orf_sb = GLB.al('orf', [1, 128], F32)
        one_sb = GLB.al('one', [1, 1], F32)
        idf_sb = GLB.al('idf', [128, 128], F32)
        exps_sb = GLB.al('expd', [8, 256], BF16)
        nbi_sb = GLB.al('nbi', [128, 128], BF16)
        wgt_sb = GLB.al('wgt', [128, NCH * 8], F32)
        bq_sb = GLB.al('bq', [128, NH], F32)
        bqs_sb = GLB.al('bqs', [128, NH], F32)
        bk_sb = GLB.al('bk', [128, NKV], F32)
        bo_sb = GLB.al('bo', [128, NCH], F32)
        sml = GLB.al('sml', [128, 16], F32)
        gat = GLB.al('gat', [128, 56], F32)
        serow = GLB.al('serow', [1, 8], F32)
        krow = GLB.al('krow', [1, 512], F32)
        cms = GLB.al('cms', [1, 16], F32)
        rw0c = GLB.al('rw0c', [128, 4], F32)
        gat2 = [GLB.al(f'gat2_{m}', [8, TD], BF16) for m in range(4)]
        dn = [GLB.al(f'dn{k}', [128, TD], F32) for k in range(2)]

        pq = [es.enter_context(nc.psum_tensor(f'pq{i}', [128, 1024], F32)) for i in range(4)]
        pb = [pq[i // 2][:, (i % 2) * 512:(i % 2) * 512 + 512] for i in range(8)]

        semnames = ['pe', 'sc', 've', 'dCon', 'dWkv', 'dX0', 'dX1', 'dM0',
                    'dM1', 'dQ0', 'dQ1', 'dO0', 'dO1', 'dXoc0', 'dXoc1', 'dSt0', 'dSt1', 'dW2', 'dC1',
                    'dXr', 'dF0', 'dF1', 'dB0', 'dB1', 'dB2', 'dA0', 'dA1',
                    'dW0', 'dW1', 'oS0', 'oS1']
        sems = {n: es.enter_context(nc.semaphore(n)) for n in semnames}
        P = _P(sems)
        blk = es.enter_context(nc.Block())

        for dst, src in ((ocb_sb, c_ocb), (ocf_sb, c_ocf), (orf_sb, c_orf)):
            P.dma('sync', dst, src, sem='dC1')
        d_c1 = P.cnt['dC1']
        for dst, src in ((one_sb, c_one), (idf_sb, c_idf), (exps_sb, c_exp),
                         (nbi_sb, c_nbi),
                         (wgt_sb, wgt_h), (bq_sb, bq_h), (bk_sb, bk_h),
                         (bo_sb, bo_h)):
            P.dma('sync', dst, src, sem='dCon')
        d_consts = P.dma('sync', bvr_sb, bv_h, sem='dCon')


        RM = slice(0, TA)
        RR = slice(TA, 2 * TA)
        R3 = slice(2 * TA, 3 * TA)
        ms = {k: [None] * (NT + 2) for k in
              ('xb', 'sq', 'sum', 'sumsq', 'pre', 'lnx', 'bc', 'rbb',
               'ap1', 'ap', 'kv', 'kev', 'vev', 'post')}

        def xln_dst(j, hc):
            if j < NTA:
                return xln_t[j % 2][:, hc * TA:(hc + 1) * TA]
            return xlnown[:, hc * TOWN + (j - NTA) * TA: hc * TOWN + (j - NTA + 1) * TA]

        for i in range(NT + 2):
            have = i < NT
            j = i - 1
            jj = i - 2
            if have:
                if i >= 2:
                    P.wait('gpsimd', 've', ms['ap'][i - 2])
                src = x_tiles[i] if i < NTA else xo_tiles[i - NTA]
                ms['xb'][i] = P.dma('gpsimd', xb_t[i % 2], src, sem=f'dX{i % 2}')
                if i == 1:
                    P.dma('gpsimd', wk_sb, wk_h, sem='dWkv')
                    d_wkv = P.dma('gpsimd', wv_sb, wv_h, sem='dWkv')

            if have:
                lnb = pb[4 + (i % 2)]
                wts = [(f'dX{i % 2}', ms['xb'][i]), ('dC1', d_c1)]
                if i >= 2:
                    wts.append(('ve', ms['pre'][i - 2]))

                def mk_sum(hc, i=i, lnb=lnb):
                    return lambda e: e.matmul(lnb[0:1, 0:TA], ocb_sb,
                                              xb_t[i % 2][:, hc * TA:(hc + 1) * TA],
                                              start=(hc == 0), stop=(hc == NCH - 1),
                                              skip_group_check=True)
                ms['sum'][i] = P.g('tensor', [mk_sum(hc) for hc in range(NCH)], waits=wts, inc='pe')

            if 0 <= j < NT:
                lnb = pb[4 + (j % 2)]
                wts = [('pe', ms['sumsq'][j])]
                if j >= 1:
                    wts.append(('pe', ms['bc'][j - 1]))
                v1 = P.g('vector', [
                    lambda e, lnb=lnb: e.tensor_scalar_mul(rowsA[0:1, RM], lnb[0:1, 0:TA], 1.0 / HID),
                    lambda e, lnb=lnb: e.tensor_scalar_mul(rowsA[0:1, RR], lnb[0:1, TA:2 * TA], 1.0 / HID),
                ], waits=wts, inc='ve')
                v2 = P.g('vector', [
                    lambda e: e.tensor_mul(rowsA[0:1, R3], rowsA[0:1, RM], rowsA[0:1, RM]),
                ], waits=[('ve', v1)], inc='ve')
                ms['pre'][j] = P.g('vector', [
                    lambda e: e.scalar_tensor_tensor(rowsA[0:1, RR], rowsA[0:1, RR], EPS,
                                                     rowsA[0:1, R3], op0=ALU.add, op1=ALU.subtract),
                ], waits=[('ve', v2)], inc='ve')

            if 0 <= j < NT:
                lnx1 = P.g('scalar', [
                    lambda e: e.activation(out=rowsA[0:1, RR], in_=rowsA[0:1, RR], func=AF.Ln),
                ], waits=[('ve', ms['pre'][j])], inc='sc')
                ms['lnx'][j] = P.g('scalar', [
                    lambda e: e.activation(out=rowsA[0:1, RR], in_=rowsA[0:1, RR], func=AF.Exp, scale=-0.5),
                ], waits=[('sc', lnx1)], inc='sc')
            if have:
                wts = [(f'dX{i % 2}', ms['xb'][i])]
                if i >= 1:
                    wts.append(('pe', ms['sumsq'][i - 1]))

                ms['sq'][i] = P.g('scalar', [
                    (lambda i=i: (lambda e: e.activation(out=sq_t, in_=xb_t[i % 2],
                                                         func=AF.Square)))()
                ], waits=wts, inc='sc')

            if 0 <= j < NT:
                ms['post'][j] = P.g('vector', [
                    lambda e: e.scalar_tensor_tensor(rowsA[0:1, R3], rowsA[0:1, RM], -1.0,
                                                     rowsA[0:1, RR], op0=ALU.mult, op1=ALU.mult),
                ], waits=[('sc', ms['lnx'][j])], inc='ve')

            if 0 <= j < NT:
                bcb = pb[6 + (j % 2)]
                wts = [('ve', ms['post'][j])]
                if j >= 2:
                    wts.append(('ve', ms['rbb'][j - 2]))
                ms['bc'][j] = P.g('tensor', [
                    lambda e, bcb=bcb: e.matmul(bcb[:, 0:TA], orf_sb, rowsA[0:1, RR],
                                                start=True, stop=True, skip_group_check=True),
                    lambda e, bcb=bcb: e.matmul(bcb[:, TA:2 * TA], orf_sb, rowsA[0:1, R3],
                                                start=True, stop=True, skip_group_check=True),
                ], waits=wts, inc='pe')

            if have:
                lnb = pb[4 + (i % 2)]

                def mk_ssq(hc, i=i, lnb=lnb):
                    return lambda e: e.matmul(lnb[0:1, TA:2 * TA], ocb_sb,
                                              sq_t[:, hc * TA:(hc + 1) * TA],
                                              start=(hc == 0), stop=(hc == NCH - 1),
                                              skip_group_check=True)
                ms['sumsq'][i] = P.g('tensor', [mk_ssq(hc) for hc in range(NCH)],
                                     waits=[('sc', ms['sq'][i])], inc='pe')

            if 0 <= j < NT:
                bcb = pb[6 + (j % 2)]
                ms['rbb'][j] = P.g('vector', [
                    lambda e, bcb=bcb, j=j: e.tensor_copy(rbb_t[j % 2], bcb[:, 0:2 * TA]),
                ], waits=[('pe', ms['bc'][j])], inc='ve')
                wts = [('ve', ms['rbb'][j])]
                if 0 <= j - 2 < NTA:
                    wts.append(('pe', ms['kv'][j - 2]))
                if j < NTA:
                    apdst = xln_t[j % 2].rearrange('p (c n) -> p c n', c=NCH)
                else:
                    apdst = xlnown.rearrange('p (c n) -> p c n', c=NCH)[
                        :, :, (j - NTA) * TA:(j - NTA + 1) * TA]
                apsrc = xb_t[j % 2].rearrange('p (c n) -> p c n', c=NCH)

                def bc_rbb(j, lo):
                    return rbb_t[j % 2][:, lo:lo + TA].rearrange(
                        'p (o n) -> p o n', o=1).to_broadcast([128, NCH, TA])
                ms['ap1'][j] = P.g('vector', [
                    (lambda j=j, apdst=apdst, apsrc=apsrc: (lambda e: e.tensor_mul(
                        apdst, apsrc, bc_rbb(j, 0))))()
                ], waits=wts, inc='ve')
                ms['ap'][j] = P.g('vector', [
                    (lambda j=j, apdst=apdst: (lambda e: e.tensor_add(
                        apdst, apdst, bc_rbb(j, TA))))()
                ], waits=[('ve', ms['ap1'][j])], inc='ve')

            if 0 <= jj < NTA:
                kvms = []
                for kv in range(NKV):
                    for hc in range(NCH):
                        def mk_k(kv=kv, hc=hc, jj=jj):
                            return lambda e: e.matmul(
                                pb[kv // 2][:, (kv % 2) * TA:(kv % 2 + 1) * TA],
                                wk_sb[:, hc * 512 + kv * 128: hc * 512 + (kv + 1) * 128],
                                xln_t[jj % 2][:, hc * TA:(hc + 1) * TA],
                                start=(hc == 0), stop=(hc == NCH - 1), skip_group_check=True)
                        kvms.append(mk_k())
                for sub in range(TA // 128):
                    for hc in range(NCH):
                        def mk_v(sub=sub, hc=hc, jj=jj):
                            return lambda e: e.matmul(
                                pb[2 + sub][:, 0:512],
                                xln_t[jj % 2][:, hc * TA + sub * 128: hc * TA + (sub + 1) * 128],
                                wv_sb[:, hc * 512:(hc + 1) * 512],
                                start=(hc == 0), stop=(hc == NCH - 1), skip_group_check=True)
                        kvms.append(mk_v())
                wts = [('ve', ms['ap'][jj]), ('dWkv', d_wkv)]
                if jj >= 1:
                    wts += [('ve', ms['kev'][jj - 1]), ('ve', ms['vev'][jj - 1])]
                ms['kv'][jj] = P.g('tensor', kvms, waits=wts, inc='pe')

            if 0 <= jj < NTA:
                def mk_kev(kv, jj=jj):
                    return lambda e: e.tensor_scalar(
                        K_sb[:, kv * S + jj * TA: kv * S + (jj + 1) * TA],
                        pb[kv // 2][:, (kv % 2) * TA:(kv % 2 + 1) * TA],
                        bk_sb[:, kv:kv + 1], None, op0=ALU.add)
                ms['kev'][jj] = P.g('vector', [mk_kev(kv) for kv in range(NKV)],
                                    waits=[('pe', ms['kv'][jj]), ('dCon', d_consts)], inc='ve')

                def mk_vev(sub, jj=jj):
                    ci = (jj * TA) // 128 + sub
                    return lambda e: e.tensor_add(V_sb[:, ci * 512:(ci + 1) * 512],
                                                  pb[2 + sub][:, 0:512], bvr_sb)
                ms['vev'][jj] = P.g('vector', [mk_vev(s) for s in range(TA // 128)],
                                    waits=[('pe', ms['kv'][jj]), ('dCon', d_consts)], inc='ve')

        a_pe, a_sc, a_ve = P.cnt['pe'], P.cnt['sc'], P.cnt['ve']

        kb_ve = None
        kb_pe_last = a_pe
        for kv in range(NKV):
            kbs = P.g('scalar', [
                (lambda kv=kv: (lambda e: e.activation(
                    out=sq_t, in_=K_sb[:, kv * S:(kv + 1) * S], func=AF.Square)))()
            ], waits=[('pe', kb_pe_last), ('sc', a_sc if kv == 0 else 0)], inc='sc')
            for rnd in range(2):
                kbm = []
                for q4i in range(4):
                    sl = rnd * 4 + q4i
                    dst = pq[q4i // 2][0:1, (q4i % 2) * 512:(q4i % 2) * 512 + 512]
                    kbm.append((lambda sl=sl, dst=dst: (lambda e: e.matmul(
                        dst, ocb_sb, sq_t[:, sl * 512:(sl + 1) * 512],
                        start=True, stop=True, skip_group_check=True)))())
                kb_pe_last = P.g('tensor', kbm,
                                 waits=[('sc', kbs)] + ([('ve', kb_ve)] if kb_ve else [('ve', a_ve)]),
                                 inc='pe')
                if kv == 0 and rnd == 0:
                    kb_ve = P.g('vector', [
                        lambda e: e.tensor_copy(krow, pq[0][0:1, 0:512]),
                    ], waits=[('pe', kb_pe_last)], inc='ve')
                else:
                    kb_ve = P.g('vector', [
                        lambda e: e.tensor_max(krow, krow, pq[0][0:1, 0:512]),
                    ], waits=[('pe', kb_pe_last), ('ve', kb_ve)], inc='ve')
                kb_ve = P.g('vector', [
                    lambda e: e.tensor_max(krow, krow, pq[0][0:1, 512:1024]),
                ], waits=[('ve', kb_ve)], inc='ve')
                kb_ve = P.g('vector', [
                    lambda e: e.tensor_max(krow, krow, pq[1][0:1, 0:512]),
                ], waits=[('ve', kb_ve)], inc='ve')
                kb_ve = P.g('vector', [
                    lambda e: e.tensor_max(krow, krow, pq[1][0:1, 512:1024]),
                ], waits=[('ve', kb_ve)], inc='ve')
            kb_ve = P.g('vector', [
                (lambda kv=kv: (lambda e: e.reduce_max(cms[0:1, kv:kv + 1], krow, axis=AXX)))()
            ], waits=[('ve', kb_ve)], inc='ve')
        kb_done = kb_ve
        kb_pe_done = kb_pe_last

        P.wait('gpsimd', 've', a_ve)
        P.wait('gpsimd', 'pe', a_pe)
        pe_q_hist = {}
        exp31_prev = None
        scg31_prev = None
        lav_prev4 = {}
        evac_prev = None
        vrB_prev = None
        for h in range(NH):
            kv = h // 4
            if h >= 2:
                P.wait('gpsimd', 'pe', pe_q_hist[h - 2])
            qdma = P.dma('gpsimd', wq_sb[h % 2], wq_h[h], sem=f'dQ{h % 2}')

            qms = []
            for half in range(2):
                for hc in range(NCH):
                    def mk_q(half=half, hc=hc, h=h):
                        return lambda e: e.matmul(
                            pq[0][:, half * 512:(half + 1) * 512],
                            wq_sb[h % 2][:, hc * 128:(hc + 1) * 128],
                            xlnown[:, hc * TOWN + half * 512: hc * TOWN + (half + 1) * 512],
                            start=(hc == 0), stop=(hc == NCH - 1), skip_group_check=True)
                    qms.append(mk_q())
            wts = [(f'dQ{h % 2}', qdma)]
            if h == 0:
                wts += [('ve', a_ve), ('sc', a_sc), ('ve', kb_done)]
            else:
                wts += [('sc', exp31_prev)]
            pe_q = P.g('tensor', qms, waits=wts, inc='pe')
            pe_q_hist[h] = pe_q

            qev = P.g('vector', [
                (lambda half, h=h: (lambda e: e.tensor_scalar(
                    q_sb[h % 2][:, half * 512:(half + 1) * 512],
                    pq[0][:, half * 512:(half + 1) * 512],
                    bq_sb[:, h:h + 1], SCALE, op0=ALU.add, op1=ALU.mult)))(half)
                for half in range(2)
            ], waits=[('pe', pe_q), ('dCon', d_consts)], inc='ve')

            qsq = P.g('scalar', [
                (lambda half, h=h: (lambda e: e.activation(
                    out=csq[:, half * 512:(half + 1) * 512],
                    in_=q_sb[h % 2][:, half * 512:(half + 1) * 512], func=AF.Square)))(half)
                for half in range(2)
            ], waits=[('ve', qev)] + ([('pe', kb_pe_done)] if h == 0 else []), inc='sc')
            pe_qs = P.g('tensor', [
                lambda e: e.matmul(pb[4][0:1, 0:512], ocb_sb, csq[:, 0:512],
                                   start=True, stop=True, skip_group_check=True),
                lambda e: e.matmul(pb[5][0:1, 0:512], ocb_sb, csq[:, 512:1024],
                                   start=True, stop=True, skip_group_check=True),
            ], waits=[('sc', qsq)] + ([('ve', vrB_prev)] if vrB_prev else []), inc='pe')
            vq = P.g('vector', [
                lambda e: e.reduce_max(cms[0:1, 4:5], pb[4][0:1, 0:512], axis=AXX),
            ], waits=[('pe', pe_qs)], inc='ve')
            vq = P.g('vector', [
                lambda e: e.reduce_max(cms[0:1, 5:6], pb[5][0:1, 0:512], axis=AXX),
            ], waits=[('ve', vq)], inc='ve')
            vq = P.g('vector', [
                lambda e: e.tensor_max(cms[0:1, 4:5], cms[0:1, 4:5], cms[0:1, 5:6]),
            ], waits=[('ve', vq)], inc='ve')
            vq = P.g('vector', [
                (lambda kv=kv: (lambda e: e.tensor_mul(cms[0:1, 6:7], cms[0:1, 4:5],
                                                       cms[0:1, kv:kv + 1])))()
            ], waits=[('ve', vq)] + ([('ve', kb_done)] if h == 0 else []), inc='ve')
            scb = P.g('scalar', [
                lambda e: e.activation(out=cms[0:1, 7:8], in_=cms[0:1, 6:7], func=AF.Ln),
            ], waits=[('ve', vq)], inc='sc')
            scb = P.g('scalar', [
                lambda e: e.activation(out=cms[0:1, 7:8], in_=cms[0:1, 7:8], func=AF.Exp, scale=0.5),
            ], waits=[('sc', scb)], inc='sc')
            vnb = P.g('vector', [
                lambda e: e.tensor_scalar_mul(cms[0:1, 8:9], cms[0:1, 7:8], -1.0),
            ], waits=[('sc', scb)], inc='ve')
            penb = P.g('tensor', [
                lambda e: e.matmul(pb[4][:, 0:1], orf_sb, cms[0:1, 8:9],
                                   start=True, stop=True, skip_group_check=True),
            ], waits=[('ve', vnb)], inc='pe')
            vnbc = P.g('vector', [
                lambda e: e.tensor_copy(sml[:, 0:1], pb[4][:, 0:1]),
            ], waits=[('pe', penb), ('sc', exp31_prev)], inc='ve')

            exp_s, lav_s, scg_s = {}, {}, {}

            def emit_sc(sx):
                rot = sx % 2
                pi = sx % 4
                bA = pb[2] if rot == 0 else pb[0]
                bB = (pb[3] if rot == 0 else pb[1]) if sx < 16 else pb[sx % 4]
                ksl = slice(kv * S + sx * 128, kv * S + (sx + 1) * 128)
                if h == 0 and sx == 0:
                    P.wait('sync', 've', a_ve)
                    P.wait('sync', 'pe', a_pe)
                P.wait('sync', 'pe', scg_s.get(sx - 2) or scg31_prev)
                mdma = P.dma('sync', mk_sb[sx % 2], masks_h[:, sx * 512:(sx + 1) * 512],
                             sem=f'dM{sx % 2}')
                scms = []
                if sx < 16:
                    scms.append((lambda ksl=ksl, h=h, bA=bA: (lambda e: e.matmul(
                        bA[:, 0:512], K_sb[:, ksl], q_sb[h % 2][:, 0:512],
                        start=True, stop=False, skip_group_check=True)))())
                    scms.append((lambda sx=sx, bA=bA: (lambda e: e.matmul(
                        bA[:, 0:512], nbi_sb, mk_sb[sx % 2],
                        start=False, stop=True, skip_group_check=True)))())
                    scms.append((lambda ksl=ksl, h=h, bB=bB: (lambda e: e.matmul(
                        bB[:, 0:512], K_sb[:, ksl], q_sb[h % 2][:, 512:1024],
                        start=True, stop=True, skip_group_check=True)))())
                else:
                    scms.append((lambda ksl=ksl, h=h, bB=bB: (lambda e: e.matmul(
                        bB[:, 0:512], K_sb[:, ksl], q_sb[h % 2][:, 512:1024],
                        start=True, stop=False, skip_group_check=True)))())
                    scms.append((lambda sx=sx, bB=bB: (lambda e: e.matmul(
                        bB[:, 0:512], nbi_sb, mk_sb[sx % 2],
                        start=False, stop=True, skip_group_check=True)))())
                wts = [(f'dM{sx % 2}', mdma)]
                if sx <= 1:
                    wts += [('ve', qev), ('ve', vnbc)]
                elif sx < 16:
                    wts.append(('sc', exp_s[sx - 2]))
                elif sx < 20:
                    wts.append(('sc', exp_s[15]))
                else:
                    wts.append(('sc', exp_s[sx - 4]))
                scg_s[sx] = P.g('tensor', scms, waits=wts, inc='pe')

                exps = []
                if sx < 16:
                    exps.append((lambda rot=rot, pi=pi: (lambda e: e.activation(
                        out=p_sb[pi][:, 0:1024], in_=pq[1 - rot][:, 0:1024], func=AF.Exp,
                        bias=sml[:, 0:1])))())
                else:
                    exps.append((lambda pi=pi, bB=bB: (lambda e: e.activation(
                        out=p_sb[pi][:, 512:1024], in_=bB[:, 0:512], func=AF.Exp,
                        bias=sml[:, 0:1])))())
                pv = lav_s.get(sx - 4) or lav_prev4.get(sx - 4 + 32)
                exp_s[sx] = P.g('scalar', exps,
                                waits=[('pe', scg_s[sx]), ('ve', vnbc)] + ([('pe', pv)] if pv else []),
                                inc='sc')

            def emit_lav(sx):
                pi = sx % 4
                lav = []
                if sx < 16:
                    lav.append((lambda pi=pi, sx=sx: (lambda e: e.matmul(
                        pb[4][0:1, 0:512], ocb_sb, p_sb[pi][:, 0:512],
                        start=(sx == 0), stop=(sx == 15), skip_group_check=True)))())
                lav.append((lambda pi=pi, sx=sx: (lambda e: e.matmul(
                    pb[5][0:1, 0:512], ocb_sb, p_sb[pi][:, 512:1024],
                    start=(sx == 0), stop=(sx == 31), skip_group_check=True)))())
                vsl = slice(sx * 512 + kv * 128, sx * 512 + (kv + 1) * 128)
                if sx < 16:
                    lav.append((lambda pi=pi, vsl=vsl, sx=sx: (lambda e: e.matmul(
                        pb[6][:, 0:512], V_sb[:, vsl], p_sb[pi][:, 0:512],
                        start=(sx == 0), stop=(sx == 15), skip_group_check=True)))())
                lav.append((lambda pi=pi, vsl=vsl, sx=sx: (lambda e: e.matmul(
                    pb[7][:, 0:512], V_sb[:, vsl], p_sb[pi][:, 512:1024],
                    start=(sx == 0), stop=(sx == 31), skip_group_check=True)))())
                wts = [('sc', exp_s[sx])]
                if sx == 0:
                    wts.append(('ve', evac_prev))
                lav_s[sx] = P.g('tensor', lav, waits=wts, inc='pe')

            for sx in range(32):
                emit_sc(sx)
                if sx >= 2:
                    emit_lav(sx - 2)
            emit_lav(30)
            emit_lav(31)

            vr = P.g('vector', [
                lambda e: e.reciprocal(recA, pb[4][0:1, 0:512]),
            ], waits=[('pe', lav_s[31])], inc='ve')
            vr = P.g('vector', [
                lambda e: e.reciprocal(recB, pb[5][0:1, 0:512]),
            ], waits=[('ve', vr)], inc='ve')
            vrB_prev = vr
            perb = P.g('tensor', [
                lambda e: e.matmul(pb[2][:, 0:512], orf_sb, recA,
                                   start=True, stop=True, skip_group_check=True),
                lambda e: e.matmul(pb[3][:, 0:512], orf_sb, recB,
                                   start=True, stop=True, skip_group_check=True),
            ], waits=[('ve', vr), ('sc', exp_s[31])], inc='pe')
            vrb = P.g('vector', [
                lambda e: e.tensor_copy(rl_sb[:, 0:512], pb[2][:, 0:512]),
                lambda e: e.tensor_copy(rl_sb[:, 512:1024], pb[3][:, 0:512]),
            ], waits=[('pe', perb)], inc='ve')
            evac_prev = P.g('vector', [
                (lambda half, h=h: (lambda e: e.tensor_mul(
                    acat[:, h * TOWN + half * 512: h * TOWN + (half + 1) * 512],
                    pb[6 + half][:, 0:512], rl_sb[:, half * 512:(half + 1) * 512])))(half)
                for half in range(2)
            ], waits=[('ve', vrb)], inc='ve')
            exp31_prev = exp_s[31]
            scg31_prev = scg_s[31]
            lav_prev4 = {k: lav_s[k] for k in (28, 29, 30, 31)}

        c_heads_pe = P.cnt['pe']

        oc_ve = [None] * NCH
        oc_st = [None] * NCH
        for oc in range(NCH):
            if oc < 2:
                P.wait('gpsimd', 'pe', c_heads_pe)
            else:
                P.wait('gpsimd', 've', oc_ve[oc - 2])
            wodma = P.dma('gpsimd', wo_sb[oc % 2], wo_h[oc], sem=f'dO{oc % 2}')
            if oc >= 2:
                P.wait('sync', 've', oc_ve[oc - 2])
            xocdma = P.dma('sync', xoc_sb[oc % 2], xo_oc[oc], sem=f'dXoc{oc % 2}')

            oms = []
            for half in range(2):
                for hc2 in range(NCH):
                    def mk_o(half=half, hc2=hc2, oc=oc):
                        return lambda e: e.matmul(
                            pb[2 * (oc % 2) + half][:, 0:512],
                            wo_sb[oc % 2][:, hc2 * 128:(hc2 + 1) * 128],
                            acat[:, hc2 * TOWN + half * 512: hc2 * TOWN + (half + 1) * 512],
                            start=(hc2 == 0), stop=(hc2 == NCH - 1), skip_group_check=True)
                    oms.append(mk_o())
            wts = [(f'dO{oc % 2}', wodma), ('ve', evac_prev)]
            if oc >= 2:
                wts.append(('ve', oc_ve[oc - 2]))
            pe_oc = P.g('tensor', oms, waits=wts, inc='pe')
            oc_ve[oc] = P.g('vector', [
                (lambda half, oc=oc: (lambda e: e.scalar_tensor_tensor(
                    ev_sb[oc % 2][:, half * 512:(half + 1) * 512],
                    pb[2 * (oc % 2) + half][:, 0:512], bo_sb[:, oc:oc + 1],
                    xoc_sb[oc % 2][:, half * 512:(half + 1) * 512],
                    op0=ALU.add, op1=ALU.add)))(half)
                for half in range(2)
            ], waits=[('pe', pe_oc), (f'dXoc{oc % 2}', xocdma),
                      (f'dSt{oc % 2}', oc_st[oc - 2] if oc >= 2 else 0)], inc='ve')
            P.wait('sync', 've', oc_ve[oc])
            for tt in range(2):
                v = P.dma('sync', xres_d[tt, :, oc, :], ev_sb[oc % 2][:, tt * TD:(tt + 1) * TD],
                          sem=f'dSt{oc % 2}')
            oc_st[oc] = v
        xres_st0, xres_st1 = P.cnt['dSt0'], P.cnt['dSt1']

        DM = slice(0, TD)
        DR = slice(TD, 2 * TD)
        D3 = slice(2 * TD, 3 * TD)
        prev_dn_ve = None
        prev_fl_pe = None
        prev_ap_ve = None
        dse_v = None
        last_st = {0: None, 1: None}
        for tt in range(2):
            P.wait('sync', 'dSt0', xres_st0)
            P.wait('sync', 'dSt1', xres_st1)
            if prev_dn_ve:
                P.wait('sync', 've', prev_dn_ve)
            xr_dma = P.dma('sync', xrt.rearrange('p (c n) -> p c n', c=NCH), xres_d[tt],
                           sem='dXr')
            P.wait('gpsimd', 'dSt0', xres_st0)
            P.wait('gpsimd', 'dSt1', xres_st1)
            if tt == 0:
                P.wait('gpsimd', 'pe', c_heads_pe)
            else:
                P.wait('gpsimd', 'pe', prev_down_pe)
            a1st = x2s[:, 8 * NCH * TD // TD * 0 + 16384:20480]
            a3st = x2s[:, 20480:24576]
            a1_dma = P.dma('gpsimd', a1st, a1_h, sem='dF0')
            a3_dma = P.dma('gpsimd', a3st, a3_h, sem='dF1')
            b2_dma = P.dma('gpsimd', b2_sb, b2_h, sem='dB2')

            wcast = [('dXr', xr_dma)]
            if prev_fl_pe:
                wcast.append(('pe', prev_fl_pe))
            if prev_dn_ve:
                wcast.append(('ve', prev_dn_ve))
            vcast = P.g('vector', [
                lambda e: e.tensor_copy(xb2a, xrt),
            ], waits=wcast, inc='ve')

            sqd_g = P.g('scalar', [
                lambda e: e.activation(out=sqda, in_=xb2a, func=AF.Square),
            ], waits=[('ve', vcast)], inc='sc')
            wts = [('ve', vcast)]
            if prev_fl_pe:
                wts.append(('pe', prev_fl_pe))
            dsum = P.g('tensor', [
                (lambda hc: (lambda e: e.matmul(pb[4][0:1, 0:TD], ocb_sb,
                                                xb2a[:, hc * TD:(hc + 1) * TD],
                                                start=(hc == 0), stop=(hc == NCH - 1),
                                                skip_group_check=True)))(hc)
                for hc in range(NCH)
            ], waits=wts, inc='pe')
            dsumsq = P.g('tensor', [
                (lambda hc: (lambda e: e.matmul(pb[5][0:1, 0:TD], ocb_sb,
                                                sqda[:, hc * TD:(hc + 1) * TD],
                                                start=(hc == 0), stop=(hc == NCH - 1),
                                                skip_group_check=True)))(hc)
                for hc in range(NCH)
            ], waits=[('sc', sqd_g)], inc='pe')
            v1 = P.g('vector', [
                lambda e: e.tensor_scalar_mul(rowsD[0:1, DM], pb[4][0:1, 0:TD], 1.0 / HID),
                lambda e: e.tensor_scalar_mul(rowsD[0:1, DR], pb[5][0:1, 0:TD], 1.0 / HID),
            ], waits=[('pe', dsumsq)], inc='ve')
            v2 = P.g('vector', [
                lambda e: e.tensor_mul(rowsD[0:1, D3], rowsD[0:1, DM], rowsD[0:1, DM]),
            ], waits=[('ve', v1)], inc='ve')
            dpre = P.g('vector', [
                lambda e: e.scalar_tensor_tensor(rowsD[0:1, DR], rowsD[0:1, DR], EPS,
                                                 rowsD[0:1, D3], op0=ALU.add, op1=ALU.subtract),
            ], waits=[('ve', v2)], inc='ve')
            dlnx = P.g('scalar', [
                lambda e: e.activation(out=rowsD[0:1, DR], in_=rowsD[0:1, DR], func=AF.Ln),
            ], waits=[('ve', dpre)], inc='sc')
            dlnx = P.g('scalar', [
                lambda e: e.activation(out=rowsD[0:1, DR], in_=rowsD[0:1, DR], func=AF.Exp, scale=-0.5),
            ], waits=[('sc', dlnx)], inc='sc')
            dpost = P.g('vector', [
                lambda e: e.scalar_tensor_tensor(rowsD[0:1, D3], rowsD[0:1, DM], -1.0,
                                                 rowsD[0:1, DR], op0=ALU.mult, op1=ALU.mult),
            ], waits=[('sc', dlnx)], inc='ve')
            dbc = P.g('tensor', [
                lambda e: e.matmul(pb[6][:, 0:TD], orf_sb, rowsD[0:1, DR],
                                   start=True, stop=True, skip_group_check=True),
                lambda e: e.matmul(pb[7][:, 0:TD], orf_sb, rowsD[0:1, D3],
                                   start=True, stop=True, skip_group_check=True),
            ], waits=[('ve', dpost)], inc='pe')
            drbb = P.g('vector', [
                lambda e: e.tensor_copy(rbb2[:, 0:TD], pb[6][:, 0:TD]),
                lambda e: e.tensor_copy(rbb2[:, TD:2 * TD], pb[7][:, 0:TD]),
            ], waits=[('pe', dbc)], inc='ve')
            xln2v = xln2.rearrange('p (c n) -> p c n', c=NCH)
            xb2av = xb2a.rearrange('p (c n) -> p c n', c=NCH)

            def bc_rbb2(lo):
                return rbb2[:, lo:lo + TD].rearrange(
                    'p (o n) -> p o n', o=1).to_broadcast([128, NCH, TD])
            dap1 = P.g('vector', [
                lambda e: e.tensor_mul(xln2v, xb2av, bc_rbb2(0)),
            ], waits=[('ve', drbb)], inc='ve')
            d_ap = P.g('vector', [
                lambda e: e.tensor_add(xln2v, xln2v, bc_rbb2(TD)),
            ], waits=[('ve', dap1)], inc='ve')
            prev_ap_ve = d_ap
            dnm = P.g('vector', [
                lambda e: e.tensor_scalar_mul(rowsD[0:1, D3], rowsD[0:1, DM], -1.0),
            ], waits=[('ve', drbb)], inc='ve')

            a13_pe = [None, None]
            for which in range(2):
                ams = []
                for eh in range(2):
                    for hc in range(NCH):
                        def mk_a(which=which, eh=eh, hc=hc):
                            bank = pb[4 + eh] if which == 0 else pb[6 + eh]
                            buf = a1st if which == 0 else a3st
                            return lambda e, bank=bank, buf=buf: e.matmul(
                                bank[:, 0:TD],
                                buf[:, hc * 256 + eh * 128: hc * 256 + (eh + 1) * 128],
                                xln2[:, hc * TD:(hc + 1) * TD],
                                start=(hc == 0), stop=(hc == NCH - 1), skip_group_check=True)
                        ams.append(mk_a())
                a13_pe[which] = P.g('tensor', ams,
                                    waits=[('ve', d_ap), ('dF0', a1_dma), ('dF1', a3_dma)], inc='pe')


            if tt == 0:
                dse = P.g('tensor', [
                    (lambda hc: (lambda e: e.matmul(pb[3][0:1, 0:8], ocf_sb,
                                                    wgt_sb[:, hc * 8:(hc + 1) * 8],
                                                    start=(hc == 0), stop=(hc == NCH - 1),
                                                    skip_group_check=True)))(hc)
                    for hc in range(NCH)
                ], waits=[('dCon', d_consts), ('pe', c_heads_pe)], inc='pe')
                dse_v = P.g('vector', [
                    lambda e: e.tensor_copy(serow, pb[3][0:1, 0:8]),
                ], waits=[('pe', dse)], inc='ve')

            ts_ve = None
            for ts in range(4):
                dms = []
                for hc in range(NCH):
                    def mk_d(hc=hc, ts=ts):
                        return lambda e: e.matmul(
                            pb[2][:, 0:8],
                            xrt[:, hc * TD + ts * 128: hc * TD + ts * 128 + 128],
                            wgt_sb[:, hc * 8:(hc + 1) * 8],
                            start=(hc == 0), stop=False, skip_group_check=True)
                    dms.append(mk_d())
                dms.append((lambda ts=ts: (lambda e: e.matmul(
                    pb[2][:, 0:8], rowsD[0:1, 2 * TD + ts * 128: 2 * TD + ts * 128 + 128],
                    serow, start=False, stop=True, skip_group_check=True)))())
                wts = [('dXr', xr_dma), ('ve', dnm), ('ve', dse_v)]
                if ts > 0:
                    wts.append(('ve', ts_ve))
                ddot = P.g('tensor', dms, waits=wts, inc='pe')
                drc = P.g('tensor', [
                    (lambda ts=ts: (lambda e: e.matmul(
                        pb[3][:, 8:9], rowsD[0:1, TD + ts * 128: TD + ts * 128 + 128],
                        one_sb, start=True, stop=True, skip_group_check=True)))()
                ], waits=[('sc', dlnx), ('dCon', d_consts)] + ([('ve', ts_ve)] if ts else []), inc='pe')
                gv = P.g('vector', [
                    lambda e: e.tensor_copy(sml[:, 1:2], pb[3][:, 8:9]),
                ], waits=[('pe', drc)], inc='ve')
                gv = P.g('vector', [
                    lambda e: e.tensor_scalar(gat[:, 0:8], pb[2][:, 0:8], sml[:, 1:2], None, op0=ALU.mult),
                ], waits=[('ve', gv), ('pe', ddot)], inc='ve')
                gv = P.g('vector', [
                    lambda e: e.max(out=gat[:, 32:40], in_=gat[:, 0:8]),
                ], waits=[('ve', gv)], inc='ve')
                gv = P.g('vector', [
                    lambda e: e.tensor_scalar(gat[:, 8:16], gat[:, 0:8], gat[:, 32:33], None, op0=ALU.is_equal),
                ], waits=[('ve', gv)], inc='ve')
                gv = P.g('vector', [
                    lambda e: e.scalar_tensor_tensor(gat[:, 16:24], gat[:, 8:16], -1e9,
                                                     gat[:, 0:8], op0=ALU.mult, op1=ALU.add),
                ], waits=[('ve', gv)], inc='ve')
                gv = P.g('vector', [
                    lambda e: e.tensor_scalar(gat[:, 24:32], gat[:, 16:24], gat[:, 33:34], None, op0=ALU.is_equal),
                    lambda e: e.tensor_sub(sml[:, 4:5], gat[:, 32:33], gat[:, 33:34]),
                ], waits=[('ve', gv)], inc='ve')
                gsc = P.g('scalar', [
                    lambda e: e.activation(out=sml[:, 5:6], in_=sml[:, 4:5], func=AF.Exp, scale=-1.0),
                ], waits=[('ve', gv)], inc='sc')
                gv = P.g('vector', [
                    lambda e: e.tensor_scalar_add(sml[:, 5:6], sml[:, 5:6], 1.0),
                ], waits=[('sc', gsc)], inc='ve')
                gv = P.g('vector', [
                    lambda e: e.reciprocal(sml[:, 2:3], sml[:, 5:6]),
                ], waits=[('ve', gv)], inc='ve')
                gv = P.g('vector', [
                    lambda e: e.tensor_scalar(sml[:, 3:4], sml[:, 2:3], -1.0, 1.0, op0=ALU.mult, op1=ALU.add),
                ], waits=[('ve', gv)], inc='ve')
                gv = P.g('vector', [
                    (lambda ts=ts: (lambda e: e.tensor_copy(rw0c[:, ts:ts + 1], sml[:, 2:3])))(),
                    lambda e: e.tensor_scalar(gat[:, 40:48], gat[:, 8:16], sml[:, 2:3], None, op0=ALU.mult),
                    lambda e: e.tensor_scalar(gat[:, 48:56], gat[:, 24:32], sml[:, 3:4], None, op0=ALU.mult),
                ], waits=[('ve', gv)], inc='ve')
                dtp = P.g('tensor', [
                    (lambda col, mi: (lambda e: e.matmul(
                        pb[3][0:8, mi * 128:(mi + 1) * 128], gat[:, col:col + 8], idf_sb,
                        start=True, stop=True, skip_group_check=True)))(col, mi)
                    for mi, col in enumerate((8, 24, 40, 48))
                ], waits=[('ve', gv)], inc='pe')
                ts_ve = P.g('vector', [
                    (lambda mi, ts=ts: (lambda e: e.tensor_copy(
                        gat2[mi][0:8, ts * 128:(ts + 1) * 128],
                        pb[3][0:8, mi * 128:(mi + 1) * 128])))(mi)
                    for mi in range(4)
                ], waits=[('pe', dtp)], inc='ve')

            prw = P.g('tensor', [
                (lambda ts: (lambda e: e.matmul(
                    pb[2][0:1, ts * 128:(ts + 1) * 128], rw0c[:, ts:ts + 1], idf_sb,
                    start=True, stop=True, skip_group_check=True)))(ts)
                for ts in range(4)
            ], waits=[('ve', ts_ve)], inc='pe')
            vrw = P.g('vector', [
                lambda e: e.tensor_copy(krow, pb[2][0:1, 0:512]),
            ], waits=[('pe', prw)], inc='ve')
            prwb = P.g('tensor', [
                lambda e: e.matmul(pb[2][:, 0:TD], orf_sb, krow,
                                   start=True, stop=True, skip_group_check=True),
            ], waits=[('ve', vrw)], inc='pe')
            vrwb = P.g('vector', [
                lambda e: e.tensor_copy(rwb0, pb[2][:, 0:TD]),
            ], waits=[('pe', prwb)], inc='ve')

            dexp_ve = None
            for mi, dstbuf in ((0, mer[0]), (1, mer[1]), (2, wmer[0]), (3, wmer[1])):
                for eh in range(2):
                    wts = [('ve', ts_ve)]
                    if dexp_ve:
                        wts.append(('ve', dexp_ve))
                    pee = P.g('tensor', [
                        (lambda mi=mi, eh=eh: (lambda e: e.matmul(
                            pb[3][:, 0:TD], exps_sb[:, eh * 128:(eh + 1) * 128],
                            gat2[mi][0:8, 0:TD],
                            start=True, stop=True, skip_group_check=True)))()
                    ], waits=wts, inc='pe')
                    dexp_ve = P.g('vector', [
                        (lambda eh=eh, dstbuf=dstbuf: (lambda e: e.tensor_copy(
                            dstbuf[:, eh * TD:(eh + 1) * TD], pb[3][:, 0:TD])))()
                    ], waits=[('pe', pee)], inc='ve')

            dc_ve = P.g('vector', [
                (lambda w, k, eh: (lambda e: e.tensor_mul(
                    (c1 if w == 0 else c3)[k][:, eh * TD:(eh + 1) * TD],
                    (pb[4 + eh] if w == 0 else pb[6 + eh])[:, 0:TD],
                    mer[k][:, eh * TD:(eh + 1) * TD])))(w, k, eh)
                for w in range(2) for k in range(2) for eh in range(2)
            ], waits=[('pe', a13_pe[1]), ('ve', dexp_ve)], inc='ve')
            dc_ve = P.g('vector', [
                (lambda eh: (lambda e: e.tensor_scalar_mul(
                    nc1[:, eh * TD:(eh + 1) * TD], c1[0][:, eh * TD:(eh + 1) * TD], -1.0)))(eh)
                for eh in range(2)
            ] + [
                (lambda eh: (lambda e: e.tensor_scalar_mul(
                    nc3[:, eh * TD:(eh + 1) * TD], c3[0][:, eh * TD:(eh + 1) * TD], -1.0)))(eh)
                for eh in range(2)
            ], waits=[('ve', dc_ve), ('ve', vrwb)], inc='ve')

            silu1_s, x2ve, a2pe, pair_pe = {}, {}, {}, {}
            dma_b13, dma_a2, w3d_s = {}, {}, {}
            hx1, hx3 = {}, {}
            vx20_s, vx21_s, a20_s, a21_s, pex3b_s = {}, {}, {}, {}, {}

            def emit_heads(fc):
                j2 = fc // 2
                u = fc % 2
                q4 = fc // 4
                if fc >= 2:
                    P.wait('gpsimd', 'pe', hx3[fc - 2])
                else:
                    P.wait('gpsimd', 've', dc_ve)
                P.dma('gpsimd', w13[fc % 2][:, 0:NCH * 128], w1_h[fc], sem=f'dF{fc % 2}')
                w3d_s[fc] = P.dma('gpsimd', w13[fc % 2][:, NCH * 128:2 * NCH * 128],
                                  w3_h[fc], sem=f'dF{fc % 2}')
                if u == 0:
                    if fc >= 3:
                        P.wait('gpsimd', 'pe', pex3b_s[fc - 3])
                    P.dma('gpsimd', b13[j2 % 2][:, 0:512], b1_h[j2], sem=f'dB{j2 % 2}')
                    dma_b13[j2] = P.dma('gpsimd', b13[j2 % 2][:, 512:1024], b3_h[j2],
                                        sem=f'dB{j2 % 2}')
                if fc % 4 == 0:
                    if fc >= 5:
                        P.wait('gpsimd', 'pe', a2pe[fc - 5])
                    dma_a2[q4] = P.dma('gpsimd', a2s[q4 % 2], a2_h[q4], sem=f'dA{q4 % 2}')

                bx1 = pb[fc % 2]
                bx3 = pb[2 + fc % 2]
                b13buf = b13[j2 % 2]

                def lw13(hc, w, fc=fc):
                    return w13[fc % 2][:, w * NCH * 128 + hc * 128: w * NCH * 128 + (hc + 1) * 128]

                def lb13(w, eh, u=u, b13buf=b13buf):
                    off = w * 512 + eh * 256 + u * 128
                    return b13buf[:, off:off + 128]

                x1ms = [(lambda hc, bx1=bx1, w=lw13: (lambda e: e.matmul(
                    bx1[:, 0:TD], w(hc, 0), xln2[:, hc * TD:(hc + 1) * TD],
                    start=(hc == 0), stop=False, skip_group_check=True)))(hc) for hc in range(NCH)]
                x1ms += [(lambda eh, bx1=bx1, b=lb13: (lambda e: e.matmul(
                    bx1[:, 0:TD], b(0, eh), c1[0][:, eh * TD:(eh + 1) * TD],
                    start=False, stop=(eh == 1), skip_group_check=True)))(eh) for eh in range(2)]
                wts = [(f'dF{fc % 2}', w3d_s[fc]), (f'dB{j2 % 2}', dma_b13[j2]), ('ve', dc_ve)]
                if fc >= 2:
                    wts.append(('sc', silu1_s[fc - 2]))
                hx1[fc] = P.g('tensor', x1ms, waits=wts, inc='pe')

                x3ms = [(lambda hc, bx3=bx3, w=lw13: (lambda e: e.matmul(
                    bx3[:, 0:TD], w(hc, 1), xln2[:, hc * TD:(hc + 1) * TD],
                    start=(hc == 0), stop=False, skip_group_check=True)))(hc) for hc in range(NCH)]
                x3ms += [(lambda eh, bx3=bx3, b=lb13: (lambda e: e.matmul(
                    bx3[:, 0:TD], b(1, eh), c3[0][:, eh * TD:(eh + 1) * TD],
                    start=False, stop=(eh == 1), skip_group_check=True)))(eh) for eh in range(2)]
                wts = []
                if fc >= 2:
                    wts.append(('ve', vx21_s[fc - 2]))
                hx3[fc] = P.g('tensor', x3ms, waits=wts, inc='pe')

            def emit_tails(fc):
                j2 = fc // 2
                u = fc % 2
                q4 = fc // 4
                par = fc % 2
                bx1 = pb[fc % 2]
                bx3 = pb[2 + fc % 2]
                b13buf = b13[j2 % 2]
                a2buf = a2s[q4 % 2]

                def lb13(w, eh, u=u, b13buf=b13buf):
                    off = w * 512 + eh * 256 + u * 128
                    return b13buf[:, off:off + 128]

                wts = [('pe', hx1[fc])]
                if fc >= 2:
                    wts.append(('ve', vx20_s[fc - 2]))
                s0 = P.g('scalar', [(lambda bx1=bx1, par=par: (lambda e: e.activation(
                    out=s_scr[0][par], in_=bx1[:, 0:TD], func=AF.Silu)))()],
                    waits=wts, inc='sc')
                x1b = [(lambda eh, bx1=bx1, b=lb13: (lambda e: e.matmul(
                    bx1[:, 0:TD], b(0, eh), nc1[:, eh * TD:(eh + 1) * TD],
                    start=False, stop=False, skip_group_check=True)))(eh) for eh in range(2)]
                x1b += [(lambda eh, bx1=bx1, b=lb13: (lambda e: e.matmul(
                    bx1[:, 0:TD], b(0, eh), c1[1][:, eh * TD:(eh + 1) * TD],
                    start=False, stop=(eh == 1), skip_group_check=True)))(eh) for eh in range(2)]
                pex1b = P.g('tensor', x1b, waits=[('sc', s0)], inc='pe')
                wts = [('pe', pex1b)]
                if fc >= 2:
                    wts.append(('ve', vx21_s[fc - 2]))
                s1 = P.g('scalar', [(lambda bx1=bx1, par=par: (lambda e: e.activation(
                    out=s_scr[1][par], in_=bx1[:, 0:TD], func=AF.Silu)))()],
                    waits=wts, inc='sc')
                silu1_s[fc] = s1
                wts = [('sc', s0), ('pe', hx3[fc])]
                if fc >= 2:
                    wts.append(('pe', a20_s[fc - 2]))
                vx20_s[fc] = P.g('vector', [(lambda bx3=bx3, par=par: (lambda e: e.tensor_mul(
                    x2_scr[0][par], s_scr[0][par], bx3[:, 0:TD])))()],
                    waits=wts, inc='ve')
                a20_s[fc] = P.g('tensor', [(lambda eh, fc=fc, a2buf=a2buf, par=par: (lambda e: e.matmul(
                    pb[4 + eh][:, 0:TD],
                    a2buf[:, (fc % 4) * 256 + eh * 128:(fc % 4) * 256 + (eh + 1) * 128],
                    x2_scr[0][par], start=(fc == 0), stop=(fc == NFC - 1),
                    skip_group_check=True)))(eh) for eh in range(2)],
                    waits=[('ve', vx20_s[fc]), (f'dA{q4 % 2}', dma_a2[q4])], inc='pe')
                x3b = [(lambda eh, bx3=bx3, b=lb13: (lambda e: e.matmul(
                    bx3[:, 0:TD], b(1, eh), nc3[:, eh * TD:(eh + 1) * TD],
                    start=False, stop=False, skip_group_check=True)))(eh) for eh in range(2)]
                x3b += [(lambda eh, bx3=bx3, b=lb13: (lambda e: e.matmul(
                    bx3[:, 0:TD], b(1, eh), c3[1][:, eh * TD:(eh + 1) * TD],
                    start=False, stop=(eh == 1), skip_group_check=True)))(eh) for eh in range(2)]
                pex3b = P.g('tensor', x3b, waits=[('ve', vx20_s[fc])], inc='pe')
                pex3b_s[fc] = pex3b
                wts = [('sc', s1), ('pe', pex3b)]
                if fc >= 2:
                    wts.append(('pe', a21_s[fc - 2]))
                vx21_s[fc] = P.g('vector', [(lambda bx3=bx3, par=par: (lambda e: e.tensor_mul(
                    x2_scr[1][par], s_scr[1][par], bx3[:, 0:TD])))()],
                    waits=wts, inc='ve')
                vd = P.g('vector', [(lambda par=par: (lambda e: e.tensor_sub(
                    d_scr[par], x2_scr[0][par], x2_scr[1][par])))()],
                    waits=[('ve', vx21_s[fc])], inc='ve')
                vd = P.g('vector', [(lambda par=par: (lambda e: e.tensor_mul(
                    d_scr[par], d_scr[par], rwb0)))()],
                    waits=[('ve', vd)], inc='ve')
                x2ve[fc] = P.g('vector', [(lambda fc=fc, par=par: (lambda e: e.tensor_add(
                    x2s[:, fc * TD:(fc + 1) * TD], d_scr[par], x2_scr[1][par])))()],
                    waits=[('ve', vd)], inc='ve')

            def emit_a21(fc):
                q4 = fc // 4
                par = fc % 2
                a2buf = a2s[q4 % 2]
                a21 = P.g('tensor', [(lambda eh, fc=fc, a2buf=a2buf, par=par: (lambda e: e.matmul(
                    pb[6 + eh][:, 0:TD],
                    a2buf[:, (fc % 4) * 256 + eh * 128:(fc % 4) * 256 + (eh + 1) * 128],
                    x2_scr[1][par], start=(fc == 0), stop=(fc == NFC - 1),
                    skip_group_check=True)))(eh) for eh in range(2)],
                    waits=[('ve', vx21_s[fc])], inc='pe')
                a21_s[fc] = a21
                a2pe[fc] = a21
                pair_pe[fc] = a21

            for fc in range(NFC):
                emit_heads(fc)
                if fc >= 1:
                    emit_tails(fc - 1)
                if fc >= 2:
                    emit_a21(fc - 2)
            emit_tails(NFC - 1)
            emit_a21(NFC - 2)
            emit_a21(NFC - 1)
            prev_fl_pe = a2pe[NFC - 1]

            c2tmp = c3[0]
            v = P.g('vector', [
                (lambda eh: (lambda e: e.tensor_mul(
                    c1[0][:, eh * TD:(eh + 1) * TD], pb[4 + eh][:, 0:TD],
                    wmer[0][:, eh * TD:(eh + 1) * TD])))(eh)
                for eh in range(2)
            ] + [
                (lambda eh: (lambda e: e.tensor_mul(
                    c2tmp[:, eh * TD:(eh + 1) * TD], pb[6 + eh][:, 0:TD],
                    wmer[1][:, eh * TD:(eh + 1) * TD])))(eh)
                for eh in range(2)
            ], waits=[('pe', prev_fl_pe), ('ve', x2ve[NFC - 1])], inc='ve')
            c2s_ve = P.g('vector', [
                (lambda eh: (lambda e: e.tensor_add(
                    c1[0][:, eh * TD:(eh + 1) * TD], c1[0][:, eh * TD:(eh + 1) * TD],
                    c2tmp[:, eh * TD:(eh + 1) * TD])))(eh)
                for eh in range(2)
            ], waits=[('ve', v)], inc='ve')

            dn_ve = [None] * NCH
            dn_st = [None] * NCH
            qgrp_pe = {}
            for hc in range(NCH):
                qpe = None
                for q in range(4):
                    idx = hc * 4 + q
                    if idx >= 3:
                        P.wait('gpsimd', 'pe', qgrp_pe[idx - 3])
                    elif tt == 0 and idx == 0:
                        P.wait('gpsimd', 've', c2s_ve)
                    elif tt == 1 and idx == 0:
                        P.wait('gpsimd', 've', c2s_ve)
                    w2dma = P.dma('gpsimd', w2s[idx % 3], w2_h[hc, q], sem=('dW0', 'dW1', 'dW2')[idx % 3])
                    dms = []
                    for jj in range(14):
                        def mk_w2(q=q, jj=jj, hc=hc, idx=idx):
                            fc2 = 14 * q + jj
                            return lambda e: e.matmul(
                                pb[hc % 2][:, 0:TD], w2s[idx % 3][:, jj * 128:(jj + 1) * 128],
                                x2s[:, fc2 * TD:(fc2 + 1) * TD],
                                start=(q == 0 and jj == 0), stop=False, skip_group_check=True)
                        dms.append(mk_w2())
                    wts = [(('dW0', 'dW1', 'dW2')[idx % 3], w2dma), ('ve', x2ve[NFC - 1])]
                    if q == 0 and hc >= 2:
                        wts.append(('ve', dn_ve[hc - 2]))
                    qpe = P.g('tensor', dms, waits=wts, inc='pe')
                    qgrp_pe[idx] = qpe
                l2 = [(lambda eh, hc=hc: (lambda e: e.matmul(
                    pb[hc % 2][:, 0:TD], b2_sb[:, eh * HID + hc * 128: eh * HID + (hc + 1) * 128],
                    c1[0][:, eh * TD:(eh + 1) * TD],
                    start=False, stop=(eh == 1), skip_group_check=True)))(eh) for eh in range(2)]
                pedn = P.g('tensor', l2, waits=[('ve', c2s_ve), ('dB2', b2_dma)], inc='pe')
                wts = [('pe', pedn), ('dXr', xr_dma),
                       (f'oS{hc % 2}', last_st[hc % 2])]
                dn_ve[hc] = P.g('vector', [
                    (lambda hc=hc: (lambda e: e.tensor_add(
                        dn[hc % 2], pb[hc % 2][:, 0:TD], xrt[:, hc * TD:(hc + 1) * TD])))()
                ], waits=wts, inc='ve')
                P.wait('sync', 've', dn_ve[hc])
                dn_st[hc] = P.dma('sync', out_d[hc, :, tt, :], dn[hc % 2], sem=f'oS{hc % 2}')
                last_st[hc % 2] = dn_st[hc]
            prev_dn_ve = dn_ve[NCH - 1]
            prev_down_pe = P.cnt['pe']

        n_out0, n_out1 = P.cnt['oS0'], P.cnt['oS1']

        @blk.sync
        def _(e):
            for f in P.streams['sync']:
                f(e)
            e.wait_ge(sems['oS0'], n_out0)
            e.wait_ge(sems['oS1'], n_out1)

        @blk.gpsimd
        def _(e):
            for f in P.streams['gpsimd']:
                f(e)

        @blk.tensor
        def _(e):
            for f in P.streams['tensor']:
                f(e)

        @blk.vector
        def _(e):
            for f in P.streams['vector']:
                f(e)

        @blk.scalar
        def _(e):
            for f in P.streams['scalar']:
                f(e)

    return nc


_CACHED = {}


def _prep(inputs):
    f32 = np.float32
    x = np.asarray(inputs['L0_attn_x'], f32)
    wq = np.asarray(inputs['L0_attn_wq'], f32)
    bq = np.asarray(inputs['L0_attn_bq'], f32)
    wk = np.asarray(inputs['L0_attn_wk'], f32)
    bk = np.asarray(inputs['L0_attn_bk'], f32)
    wv = np.asarray(inputs['L0_attn_wv'], f32)
    bv = np.asarray(inputs['L0_attn_bv'], f32)
    wo = np.asarray(inputs['L0_attn_wo'], f32)
    bo = np.asarray(inputs['L0_attn_bo'], f32)
    Wg = np.asarray(inputs['L0_ffn_Wg'], f32)
    W1 = np.asarray(inputs['L0_ffn_W1'], f32)
    W2 = np.asarray(inputs['L0_ffn_W2'], f32)
    W3 = np.asarray(inputs['L0_ffn_W3'], f32)
    A1 = np.asarray(inputs['L0_ffn_A1'], f32)
    B1 = np.asarray(inputs['L0_ffn_B1'], f32)
    A2 = np.asarray(inputs['L0_ffn_A2'], f32)
    B2 = np.asarray(inputs['L0_ffn_B2'], f32)
    A3 = np.asarray(inputs['L0_ffn_A3'], f32)
    B3 = np.asarray(inputs['L0_ffn_B3'], f32)

    C = np.ascontiguousarray
    sh = {}
    sh['wq_h'] = C(wq.reshape(NCH, 128, NH, 128).transpose(2, 1, 0, 3).reshape(NH, 128, NCH * 128))
    sh['wk_h'] = C(wk.reshape(NCH, 128, 512).transpose(1, 0, 2).reshape(128, NCH * 512))
    sh['wv_h'] = C(wv.reshape(NCH, 128, 512).transpose(1, 0, 2).reshape(128, NCH * 512))
    sh['wo_h'] = C(wo.reshape(NCH, 128, NCH, 128).transpose(2, 1, 0, 3).reshape(NCH, 128, NCH * 128))
    sh['bq_h'] = C(bq.reshape(NH, 128).T)
    sh['bk_h'] = C(bk.reshape(NKV, 128).T)
    sh['bo_h'] = C(bo.reshape(NCH, 128).T)
    sh['bv_h'] = C(np.broadcast_to(bv[None, :], (128, 512)))
    sh['wgt_h'] = C(Wg.T.reshape(NCH, 128, 8).transpose(1, 0, 2).reshape(128, NCH * 8))
    sh['w1_h'] = C(W1.T.reshape(NCH, 128, NFC, 128).transpose(2, 1, 0, 3).reshape(NFC, 128, NCH * 128))
    sh['w3_h'] = C(W3.T.reshape(NCH, 128, NFC, 128).transpose(2, 1, 0, 3).reshape(NFC, 128, NCH * 128))
    sh['w2_h'] = C(W2.T.reshape(4, 14, 128, NCH, 128).transpose(3, 0, 2, 1, 4).reshape(NCH, 4, 128, 14 * 128))
    A1pp = A1.reshape(256, HID)
    A3pp = A3.reshape(256, HID)
    sh['a1_h'] = C(A1pp.reshape(256, NCH, 128).transpose(2, 1, 0).reshape(128, NCH * 256))
    sh['a3_h'] = C(A3pp.reshape(256, NCH, 128).transpose(2, 1, 0).reshape(128, NCH * 256))
    B1r = B1.transpose(0, 2, 1).reshape(256, FF)
    B3r = B3.transpose(0, 2, 1).reshape(256, FF)
    sh['b1_h'] = C(B1r.reshape(2, 128, 28, 256).transpose(2, 1, 0, 3).reshape(28, 128, 2 * 256))
    sh['b3_h'] = C(B3r.reshape(2, 128, 28, 256).transpose(2, 1, 0, 3).reshape(28, 128, 2 * 256))
    A2pp = A2.reshape(256, FF)
    sh['a2_h'] = C(A2pp.reshape(256, 14, 4, 128).transpose(1, 3, 2, 0).reshape(14, 128, 4 * 256))
    B2r = B2.transpose(0, 2, 1).reshape(256, HID)
    sh['b2_h'] = C(B2r.reshape(2, 128, HID).transpose(1, 0, 2).reshape(128, 2 * HID))

    in_maps = []
    ki = np.arange(128)
    qi = np.arange(512)
    for core in range(8):
        b = core // 4
        c = core % 4
        m = dict(sh)
        xT = C(x[b].T)
        m['x_tiles'] = C(xT.reshape(NCH, 128, NTA, TA).transpose(2, 1, 0, 3).reshape(NTA, 128, NCH * TA))
        own = C(np.concatenate([xT[:, 512 * c:512 * (c + 1)],
                                xT[:, 512 * (7 - c):512 * (8 - c)]], axis=1))
        m['xo_tiles'] = C(own.reshape(NCH, 128, NTB, TA).transpose(2, 1, 0, 3).reshape(NTB, 128, NCH * TA))
        m['xo_oc'] = C(own.reshape(NCH, 128, TOWN))
        mkk = np.zeros((128, 32, 512), np.float32)
        for sx in range(32):
            qpos = 512 * c + qi if sx < 16 else 512 * (7 - c) + qi
            kpos = 128 * sx + ki
            mkk[:, sx, :] = (qpos[None, :] < kpos[:, None]).astype(np.float32)
        m['masks_h'] = mkk.reshape(128, 32 * 512).astype(ml_dtypes.bfloat16)
        in_maps.append(m)
    return in_maps


def kernel(**inputs):
    in_maps = _prep(inputs)
    if 'nc' not in _CACHED:
        _CACHED['nc'] = build_program()
    nc = _CACHED['nc']
    res = run_bass_kernel_spmd(nc, in_maps, core_ids=list(range(8)))
    out = np.empty((2, HID, S), np.float32)
    for core in range(8):
        b, c = core // 4, core % 4
        o = np.asarray(res.results[core]['out_d'], np.float32)
        out[b, :, 512 * c:512 * (c + 1)] = o[:, :, 0, :].reshape(HID, TD)
        out[b, :, 512 * (7 - c):512 * (8 - c)] = o[:, :, 1, :].reshape(HID, TD)
    return np.ascontiguousarray(out.transpose(0, 2, 1))


# version 08 (final)
# speedup vs baseline: 1.0488x; 1.0488x over previous
import sys
sys.path.insert(0, '/opt/trn_rl_repo')
import numpy as np
import ml_dtypes
from contextlib import ExitStack

import concourse.bass as bass
import concourse.mybir as mybir
from concourse.bass_utils import run_bass_kernel_spmd

F32 = mybir.dt.float32
BF16 = mybir.dt.bfloat16
AF = mybir.ActivationFunctionType
ALU = mybir.AluOpType
AXX = mybir.AxisListType.X

HID = 2048
NCH = 16
S = 4096
TOWN = 1024
NH = 16
NKV = 4
FF = 7168
NFC = 56
SCALE = 1.0 / float(np.sqrt(128.0))
EPS = 1e-5
TA = 256
NTA = S // TA
NTB = TOWN // TA
NT = NTA + NTB
TD = 512
KBB = 1024

ENGS = ('tensor', 'vector', 'scalar', 'sync', 'gpsimd')
SB0 = 17536
SB_WORK = SB0 + 128 * 1024
SB_GLB = 216064
SB_END = 229376


class Arena:
    def __init__(self, nc, base, cap, tag):
        self.nc, self.off, self.cap, self.tag = nc, base, cap, tag

    def al(self, name, shape, dtype):
        bytes_pp = int(np.prod(shape[1:])) * (2 if dtype == BF16 else 4)
        off = (self.off + 31) // 32 * 32
        assert off + bytes_pp <= self.cap, (self.tag, name, off, bytes_pp, self.cap)
        self.off = off + bytes_pp
        t = self.nc.alloc_sbuf_tensor_at(f'{self.tag}_{name}', list(shape), dtype, offset=off)
        return t.ap()


class _P:
    def __init__(self, sems):
        self.sems = sems
        self.streams = {k: [] for k in ENGS}
        self.cnt = {k: 0 for k in sems}

    def bump(self, sem, n=1):
        self.cnt[sem] += n
        return self.cnt[sem]

    def dma(self, q, dst, src, sem=None):
        assert sem is not None
        v = self.bump(sem, 16)
        s = self.sems[sem]

        def run(e, dst=dst, src=src, s=s):
            e.dma_start(out=dst, in_=src).then_inc(s, 16)
        self.streams[q].append(run)
        return v

    def wait(self, eng, sem, v):
        if not v:
            return
        s = self.sems[sem]
        self.streams[eng].append(lambda e, s=s, v=v: e.wait_ge(s, v))

    def g(self, eng, builders, waits=(), inc=None, incn=1):
        ws = tuple((s, v) for (s, v) in waits if v)
        vinc = self.bump(inc, incn) if inc else None
        sems = self.sems

        def run(e, builders=tuple(builders), ws=ws, inc=inc, incn=incn):
            for sname, v in ws:
                e.wait_ge(sems[sname], v)
            last = None
            for f in builders:
                last = f(e)
            if inc is not None:
                last.then_inc(sems[inc], incn)
        self.streams[eng].append(run)
        return vinc


def build_program():
    nc = bass.Bass('TRN2', target_bir_lowering=False, debug=False)

    def din(name, shape, dtype=F32):
        return nc.dram_tensor(name, list(shape), dtype, kind='ExternalInput').ap()

    x_tiles = din('x_tiles', [NTA, 128, NCH * TA])
    xo_tiles = din('xo_tiles', [NTB, 128, NCH * TA])
    xo_oc = din('xo_oc', [NCH, 128, TOWN])
    wq_h = din('wq_h', [NH, 128, NCH * 128])
    wk_h = din('wk_h', [128, NCH * 512])
    wv_h = din('wv_h', [128, NCH * 512])
    wo_h = din('wo_h', [NCH, 128, NCH * 128])
    bq_h = din('bq_h', [128, NH])
    bk_h = din('bk_h', [128, NKV])
    bo_h = din('bo_h', [128, NCH])
    bv_h = din('bv_h', [128, 512])
    masks_h = din('masks_h', [128, 32 * 512], BF16)
    wgt_h = din('wgt_h', [128, NCH * 8])
    w1_h = din('w1_h', [NFC, 128, NCH * 128])
    w3_h = din('w3_h', [NFC, 128, NCH * 128])
    w2_h = din('w2_h', [NCH, 4, 128, 14 * 128])
    a1_h = din('a1_h', [128, NCH * 256])
    a3_h = din('a3_h', [128, NCH * 256])
    b1_h = din('b1_h', [28, 128, 2 * 256])
    b3_h = din('b3_h', [28, 128, 2 * 256])
    a2_h = din('a2_h', [14, 128, 4 * 256])
    b2_h = din('b2_h', [128, 2 * HID])

    out_d = nc.dram_tensor('out_d', [NCH, 128, 2, TD], F32, kind='ExternalOutput').ap()
    xres_d = nc.dram_tensor('xres_d', [2, 128, NCH, TD], F32, kind='Internal').ap()

    c_ocb = nc.inline_tensor(np.ones((128, 1), ml_dtypes.bfloat16), name='c_ocb').ap()
    c_ocf = nc.inline_tensor(np.ones((128, 1), np.float32), name='c_ocf').ap()
    c_orf = nc.inline_tensor(np.ones((1, 128), np.float32), name='c_orf').ap()
    c_one = nc.inline_tensor(np.ones((1, 1), np.float32), name='c_one').ap()
    c_idf = nc.inline_tensor(np.eye(128, dtype=np.float32), name='c_idf').ap()
    c_nbi = nc.inline_tensor((np.eye(128, dtype=np.float32) * -60.0).astype(ml_dtypes.bfloat16),
                             name='c_nbi').ap()
    _exp = np.zeros((8, 256), np.float32)
    for ei in range(8):
        _exp[ei, ei * 32:(ei + 1) * 32] = 1.0
    c_exp = nc.inline_tensor(_exp.astype(ml_dtypes.bfloat16), name='c_exp').ap()

    es = ExitStack()
    with es:
        PP = Arena(nc, SB0, SB_WORK, 'pp')
        K_sb = PP.al('K', [128, NKV * S], BF16)
        V_sb = PP.al('V', [128, (S // 128) * 512], BF16)
        xlnown = PP.al('xlnown', [128, NCH * TOWN], BF16)
        acat = PP.al('acat', [128, NH * TOWN], BF16)
        WKV = Arena(nc, SB0 + 96 * KBB, SB_WORK, 'wkv')
        wk_sb = WKV.al('wk', [128, NCH * 512], BF16)
        wv_sb = WKV.al('wv', [128, NCH * 512], BF16)

        WA = Arena(nc, SB_WORK, SB_GLB, 'wa')
        xb_t = [WA.al(f'xb{i}', [128, NCH * TA], BF16) for i in range(2)]
        xln_t = [WA.al(f'xln{i}', [128, NCH * TA], BF16) for i in range(2)]
        sq_t = WA.al('sq', [128, NCH * TA], BF16)
        bvr_sb = WA.al('bvr', [128, 512], F32)
        rbb_t = [WA.al(f'rbb{i}', [128, 2 * TA], BF16) for i in range(2)]
        rowsA = WA.al('rowsA', [1, 3 * TA], F32)

        WC = Arena(nc, SB_WORK, SB_GLB, 'wc')
        wq_sb = [WC.al(f'wq{i}', [128, NCH * 128], BF16) for i in range(2)]
        q_sb = [WC.al(f'q{i}', [128, TOWN], BF16) for i in range(2)]
        p_sb = [WC.al(f'p{i}', [128, TOWN], BF16) for i in range(4)]
        rl_sb = WC.al('rl', [128, TOWN], F32)
        csq = WC.al('csq', [128, TOWN], BF16)
        mk_sb = [WC.al(f'mk{i}', [128, 512], BF16) for i in range(2)]
        recA = WC.al('recA', [1, 512], F32)
        recB = WC.al('recB', [1, 512], F32)
        mko = WC.off
        wo_sb = [WC.al(f'wo{i}', [128, NCH * 128], BF16) for i in range(2)]
        xoc_sb = [WC.al(f'xoc{i}', [128, TOWN], F32) for i in range(2)]
        ev_sb = [WC.al(f'ev{i}', [128, TOWN], F32) for i in range(2)]
        WCM = Arena(nc, mko, 216064, 'wcm')
        mkall = WCM.al('mkall', [128, 32 * 512], BF16)

        WD = Arena(nc, SB_WORK, SB_GLB, 'wd')
        w13 = [WD.al(f'w13_{i}', [128, 2 * NCH * 128], BF16) for i in range(2)]
        xln2 = WD.al('xln2', [128, NCH * TD], BF16)
        b13 = [WD.al(f'b13_{i}', [128, 2 * 512], BF16) for i in range(2)]
        a2s = [WD.al(f'a2s_{i}', [128, 4 * 256], BF16) for i in range(2)]
        w2s = [WD.al(f'w2s_{i}', [128, 14 * 128], BF16) for i in range(3)]
        b2_sb = WD.al('b2', [128, 2 * HID], BF16)
        rowsD = WD.al('rowsD', [1, 3 * TD], F32)

        DP = Arena(nc, SB0, SB_WORK, 'dp')
        x2s = DP.al('x2s', [128, NFC * TD], BF16)
        xrt = DP.al('xrt', [128, NCH * TD], F32)
        c1 = [DP.al(f'c1_{k}', [128, 2 * TD], BF16) for k in range(2)]
        c3 = [DP.al(f'c3_{k}', [128, 2 * TD], BF16) for k in range(2)]
        nc1 = DP.al('nc1', [128, 2 * TD], BF16)
        nc3 = DP.al('nc3', [128, 2 * TD], BF16)
        mer = [DP.al(f'mer{k}', [128, 2 * TD], BF16) for k in range(2)]
        wmer = [DP.al(f'wmer{k}', [128, 2 * TD], BF16) for k in range(2)]
        s_scr = [[DP.al(f's{k}_{par}', [128, TD], BF16) for par in range(2)] for k in range(2)]
        x2_scr = [[DP.al(f'x2_{k}_{par}', [128, TD], BF16) for par in range(2)] for k in range(2)]
        d_scr = [DP.al(f'dd{par}', [128, TD], BF16) for par in range(2)]
        rwb0 = DP.al('rwb0', [128, TD], BF16)
        rbb2 = DP.al('rbb2', [128, 2 * TD], BF16)
        xb2a = x2s[:, 0:NCH * TD]
        sqda = x2s[:, NCH * TD:2 * NCH * TD]

        GLB = Arena(nc, SB_GLB, SB_END, 'gl')
        ocb_sb = GLB.al('ocb', [128, 1], BF16)
        ocf_sb = GLB.al('ocf', [128, 1], F32)
        orf_sb = GLB.al('orf', [1, 128], F32)
        one_sb = GLB.al('one', [1, 1], F32)
        idf_sb = GLB.al('idf', [128, 128], F32)
        exps_sb = GLB.al('expd', [8, 256], BF16)
        nbi_sb = GLB.al('nbi', [128, 128], BF16)
        wgt_sb = GLB.al('wgt', [128, NCH * 8], F32)
        bq_sb = GLB.al('bq', [128, NH], F32)
        bqs_sb = GLB.al('bqs', [128, NH], F32)
        bk_sb = GLB.al('bk', [128, NKV], F32)
        bo_sb = GLB.al('bo', [128, NCH], F32)
        sml = GLB.al('sml', [128, 16], F32)
        gat = GLB.al('gat', [128, 56], F32)
        serow = GLB.al('serow', [1, 8], F32)
        krow = GLB.al('krow', [1, 512], F32)
        cms = GLB.al('cms', [1, 16], F32)
        rw0c = GLB.al('rw0c', [128, 4], F32)
        gat2 = [GLB.al(f'gat2_{m}', [8, TD], BF16) for m in range(4)]
        dn = [GLB.al(f'dn{k}', [128, TD], F32) for k in range(2)]

        pq = [es.enter_context(nc.psum_tensor(f'pq{i}', [128, 1024], F32)) for i in range(4)]
        pb = [pq[i // 2][:, (i % 2) * 512:(i % 2) * 512 + 512] for i in range(8)]

        semnames = ['pe', 'sc', 've', 'dCon', 'dWkv', 'dX0', 'dX1', 'dM0',
                    'dM1', 'dQ0', 'dQ1', 'dO0', 'dO1', 'dXoc0', 'dXoc1', 'dSt0', 'dSt1', 'dW2', 'dC1',
                    'dXr', 'dF0', 'dF1', 'dB0', 'dB1', 'dB2', 'dA0', 'dA1',
                    'dW0', 'dW1', 'oS0', 'oS1']
        sems = {n: es.enter_context(nc.semaphore(n)) for n in semnames}
        P = _P(sems)
        blk = es.enter_context(nc.Block())

        for dst, src in ((ocb_sb, c_ocb), (ocf_sb, c_ocf), (orf_sb, c_orf)):
            P.dma('sync', dst, src, sem='dC1')
        d_c1 = P.cnt['dC1']
        for dst, src in ((one_sb, c_one), (idf_sb, c_idf), (exps_sb, c_exp),
                         (nbi_sb, c_nbi),
                         (wgt_sb, wgt_h), (bq_sb, bq_h), (bk_sb, bk_h),
                         (bo_sb, bo_h)):
            P.dma('sync', dst, src, sem='dCon')
        d_consts = P.dma('sync', bvr_sb, bv_h, sem='dCon')


        RM = slice(0, TA)
        RR = slice(TA, 2 * TA)
        R3 = slice(2 * TA, 3 * TA)
        ms = {k: [None] * (NT + 2) for k in
              ('xb', 'sq', 'sum', 'sumsq', 'pre', 'lnx', 'bc', 'rbb',
               'ap1', 'ap', 'kv', 'kev', 'vev', 'post')}

        def xln_dst(j, hc):
            if j < NTA:
                return xln_t[j % 2][:, hc * TA:(hc + 1) * TA]
            return xlnown[:, hc * TOWN + (j - NTA) * TA: hc * TOWN + (j - NTA + 1) * TA]

        for i in range(NT + 2):
            have = i < NT
            j = i - 1
            jj = i - 2
            if have:
                if i >= 2:
                    P.wait('gpsimd', 've', ms['ap'][i - 2])
                src = x_tiles[i] if i < NTA else xo_tiles[i - NTA]
                ms['xb'][i] = P.dma('gpsimd', xb_t[i % 2], src, sem=f'dX{i % 2}')
                if i == 1:
                    P.dma('gpsimd', wk_sb, wk_h, sem='dWkv')
                    d_wkv = P.dma('gpsimd', wv_sb, wv_h, sem='dWkv')

            if have:
                lnb = pb[4 + (i % 2)]
                wts = [(f'dX{i % 2}', ms['xb'][i]), ('dC1', d_c1)]
                if i >= 2:
                    wts.append(('ve', ms['pre'][i - 2]))

                def mk_sum(hc, i=i, lnb=lnb):
                    return lambda e: e.matmul(lnb[0:1, 0:TA], ocb_sb,
                                              xb_t[i % 2][:, hc * TA:(hc + 1) * TA],
                                              start=(hc == 0), stop=(hc == NCH - 1),
                                              skip_group_check=True)
                ms['sum'][i] = P.g('tensor', [mk_sum(hc) for hc in range(NCH)], waits=wts, inc='pe')

            if 0 <= j < NT:
                lnb = pb[4 + (j % 2)]
                wts = [('pe', ms['sumsq'][j])]
                if j >= 1:
                    wts.append(('pe', ms['bc'][j - 1]))
                v1 = P.g('vector', [
                    lambda e, lnb=lnb: e.tensor_scalar_mul(rowsA[0:1, RM], lnb[0:1, 0:TA], 1.0 / HID),
                    lambda e, lnb=lnb: e.tensor_scalar_mul(rowsA[0:1, RR], lnb[0:1, TA:2 * TA], 1.0 / HID),
                ], waits=wts, inc='ve')
                v2 = P.g('vector', [
                    lambda e: e.tensor_mul(rowsA[0:1, R3], rowsA[0:1, RM], rowsA[0:1, RM]),
                ], waits=[('ve', v1)], inc='ve')
                ms['pre'][j] = P.g('vector', [
                    lambda e: e.scalar_tensor_tensor(rowsA[0:1, RR], rowsA[0:1, RR], EPS,
                                                     rowsA[0:1, R3], op0=ALU.add, op1=ALU.subtract),
                ], waits=[('ve', v2)], inc='ve')

            if 0 <= j < NT:
                lnx1 = P.g('scalar', [
                    lambda e: e.activation(out=rowsA[0:1, RR], in_=rowsA[0:1, RR], func=AF.Ln),
                ], waits=[('ve', ms['pre'][j])], inc='sc')
                ms['lnx'][j] = P.g('scalar', [
                    lambda e: e.activation(out=rowsA[0:1, RR], in_=rowsA[0:1, RR], func=AF.Exp, scale=-0.5),
                ], waits=[('sc', lnx1)], inc='sc')
            if have:
                wts = [(f'dX{i % 2}', ms['xb'][i])]
                if i >= 1:
                    wts.append(('pe', ms['sumsq'][i - 1]))

                ms['sq'][i] = P.g('scalar', [
                    (lambda i=i: (lambda e: e.activation(out=sq_t, in_=xb_t[i % 2],
                                                         func=AF.Square)))()
                ], waits=wts, inc='sc')

            if 0 <= j < NT:
                ms['post'][j] = P.g('vector', [
                    lambda e: e.scalar_tensor_tensor(rowsA[0:1, R3], rowsA[0:1, RM], -1.0,
                                                     rowsA[0:1, RR], op0=ALU.mult, op1=ALU.mult),
                ], waits=[('sc', ms['lnx'][j])], inc='ve')

            if 0 <= j < NT:
                bcb = pb[6 + (j % 2)]
                wts = [('ve', ms['post'][j])]
                if j >= 2:
                    wts.append(('ve', ms['rbb'][j - 2]))
                ms['bc'][j] = P.g('tensor', [
                    lambda e, bcb=bcb: e.matmul(bcb[:, 0:TA], orf_sb, rowsA[0:1, RR],
                                                start=True, stop=True, skip_group_check=True),
                    lambda e, bcb=bcb: e.matmul(bcb[:, TA:2 * TA], orf_sb, rowsA[0:1, R3],
                                                start=True, stop=True, skip_group_check=True),
                ], waits=wts, inc='pe')

            if have:
                lnb = pb[4 + (i % 2)]

                def mk_ssq(hc, i=i, lnb=lnb):
                    return lambda e: e.matmul(lnb[0:1, TA:2 * TA], ocb_sb,
                                              sq_t[:, hc * TA:(hc + 1) * TA],
                                              start=(hc == 0), stop=(hc == NCH - 1),
                                              skip_group_check=True)
                ms['sumsq'][i] = P.g('tensor', [mk_ssq(hc) for hc in range(NCH)],
                                     waits=[('sc', ms['sq'][i])], inc='pe')

            if 0 <= j < NT:
                bcb = pb[6 + (j % 2)]
                ms['rbb'][j] = P.g('vector', [
                    lambda e, bcb=bcb, j=j: e.tensor_copy(rbb_t[j % 2], bcb[:, 0:2 * TA]),
                ], waits=[('pe', ms['bc'][j])], inc='ve')
                wts = [('ve', ms['rbb'][j])]
                if 0 <= j - 2 < NTA:
                    wts.append(('pe', ms['kv'][j - 2]))
                if j < NTA:
                    apdst = xln_t[j % 2].rearrange('p (c n) -> p c n', c=NCH)
                else:
                    apdst = xlnown.rearrange('p (c n) -> p c n', c=NCH)[
                        :, :, (j - NTA) * TA:(j - NTA + 1) * TA]
                apsrc = xb_t[j % 2].rearrange('p (c n) -> p c n', c=NCH)

                def bc_rbb(j, lo):
                    return rbb_t[j % 2][:, lo:lo + TA].rearrange(
                        'p (o n) -> p o n', o=1).to_broadcast([128, NCH, TA])
                ms['ap1'][j] = P.g('vector', [
                    (lambda j=j, apdst=apdst, apsrc=apsrc: (lambda e: e.tensor_mul(
                        apdst, apsrc, bc_rbb(j, 0))))()
                ], waits=wts, inc='ve')
                ms['ap'][j] = P.g('vector', [
                    (lambda j=j, apdst=apdst: (lambda e: e.tensor_add(
                        apdst, apdst, bc_rbb(j, TA))))()
                ], waits=[('ve', ms['ap1'][j])], inc='ve')

            if 0 <= jj < NTA:
                kvms = []
                for kv in range(NKV):
                    for hc in range(NCH):
                        def mk_k(kv=kv, hc=hc, jj=jj):
                            return lambda e: e.matmul(
                                pb[kv // 2][:, (kv % 2) * TA:(kv % 2 + 1) * TA],
                                wk_sb[:, hc * 512 + kv * 128: hc * 512 + (kv + 1) * 128],
                                xln_t[jj % 2][:, hc * TA:(hc + 1) * TA],
                                start=(hc == 0), stop=(hc == NCH - 1), skip_group_check=True)
                        kvms.append(mk_k())
                for sub in range(TA // 128):
                    for hc in range(NCH):
                        def mk_v(sub=sub, hc=hc, jj=jj):
                            return lambda e: e.matmul(
                                pb[2 + sub][:, 0:512],
                                xln_t[jj % 2][:, hc * TA + sub * 128: hc * TA + (sub + 1) * 128],
                                wv_sb[:, hc * 512:(hc + 1) * 512],
                                start=(hc == 0), stop=(hc == NCH - 1), skip_group_check=True)
                        kvms.append(mk_v())
                wts = [('ve', ms['ap'][jj]), ('dWkv', d_wkv)]
                if jj >= 1:
                    wts += [('ve', ms['kev'][jj - 1]), ('ve', ms['vev'][jj - 1])]
                ms['kv'][jj] = P.g('tensor', kvms, waits=wts, inc='pe')

            if 0 <= jj < NTA:
                def mk_kev(kv, jj=jj):
                    return lambda e: e.tensor_scalar(
                        K_sb[:, kv * S + jj * TA: kv * S + (jj + 1) * TA],
                        pb[kv // 2][:, (kv % 2) * TA:(kv % 2 + 1) * TA],
                        bk_sb[:, kv:kv + 1], None, op0=ALU.add)
                ms['kev'][jj] = P.g('vector', [mk_kev(kv) for kv in range(NKV)],
                                    waits=[('pe', ms['kv'][jj]), ('dCon', d_consts)], inc='ve')

                def mk_vev(sub, jj=jj):
                    ci = (jj * TA) // 128 + sub
                    return lambda e: e.tensor_add(V_sb[:, ci * 512:(ci + 1) * 512],
                                                  pb[2 + sub][:, 0:512], bvr_sb)
                ms['vev'][jj] = P.g('vector', [mk_vev(s) for s in range(TA // 128)],
                                    waits=[('pe', ms['kv'][jj]), ('dCon', d_consts)], inc='ve')

        a_pe, a_sc, a_ve = P.cnt['pe'], P.cnt['sc'], P.cnt['ve']

        kb_ve = None
        kb_pe_last = a_pe
        for kv in range(NKV):
            kbs = P.g('scalar', [
                (lambda kv=kv: (lambda e: e.activation(
                    out=sq_t, in_=K_sb[:, kv * S:(kv + 1) * S], func=AF.Square)))()
            ], waits=[('pe', kb_pe_last), ('sc', a_sc if kv == 0 else 0)], inc='sc')
            for rnd in range(2):
                kbm = []
                for q4i in range(4):
                    sl = rnd * 4 + q4i
                    dst = pq[q4i // 2][0:1, (q4i % 2) * 512:(q4i % 2) * 512 + 512]
                    kbm.append((lambda sl=sl, dst=dst: (lambda e: e.matmul(
                        dst, ocb_sb, sq_t[:, sl * 512:(sl + 1) * 512],
                        start=True, stop=True, skip_group_check=True)))())
                kb_pe_last = P.g('tensor', kbm,
                                 waits=[('sc', kbs)] + ([('ve', kb_ve)] if kb_ve else [('ve', a_ve)]),
                                 inc='pe')
                if kv == 0 and rnd == 0:
                    kb_ve = P.g('vector', [
                        lambda e: e.tensor_copy(krow, pq[0][0:1, 0:512]),
                    ], waits=[('pe', kb_pe_last)], inc='ve')
                else:
                    kb_ve = P.g('vector', [
                        lambda e: e.tensor_max(krow, krow, pq[0][0:1, 0:512]),
                    ], waits=[('pe', kb_pe_last), ('ve', kb_ve)], inc='ve')
                kb_ve = P.g('vector', [
                    lambda e: e.tensor_max(krow, krow, pq[0][0:1, 512:1024]),
                ], waits=[('ve', kb_ve)], inc='ve')
                kb_ve = P.g('vector', [
                    lambda e: e.tensor_max(krow, krow, pq[1][0:1, 0:512]),
                ], waits=[('ve', kb_ve)], inc='ve')
                kb_ve = P.g('vector', [
                    lambda e: e.tensor_max(krow, krow, pq[1][0:1, 512:1024]),
                ], waits=[('ve', kb_ve)], inc='ve')
            kb_ve = P.g('vector', [
                (lambda kv=kv: (lambda e: e.reduce_max(cms[0:1, kv:kv + 1], krow, axis=AXX)))()
            ], waits=[('ve', kb_ve)], inc='ve')
        kb_done = kb_ve
        kb_pe_done = kb_pe_last
        P.wait('sync', 'pe', kb_pe_done)
        P.wait('sync', 've', a_ve)
        d_mk = P.dma('sync', mkall, masks_h, sem='dM0')

        P.wait('gpsimd', 've', a_ve)
        P.wait('gpsimd', 'pe', a_pe)
        pe_q_hist = {}
        exp31_prev = None
        scg31_prev = None
        lav_prev4 = {}
        evac_prev = None
        vrB_prev = None
        for h in range(NH):
            kv = h // 4
            if h >= 2:
                P.wait('gpsimd', 'pe', pe_q_hist[h - 2])
            qdma = P.dma('gpsimd', wq_sb[h % 2], wq_h[h], sem=f'dQ{h % 2}')

            qms = []
            for half in range(2):
                for hc in range(NCH):
                    def mk_q(half=half, hc=hc, h=h):
                        return lambda e: e.matmul(
                            pq[0][:, half * 512:(half + 1) * 512],
                            wq_sb[h % 2][:, hc * 128:(hc + 1) * 128],
                            xlnown[:, hc * TOWN + half * 512: hc * TOWN + (half + 1) * 512],
                            start=(hc == 0), stop=(hc == NCH - 1), skip_group_check=True)
                    qms.append(mk_q())
            wts = [(f'dQ{h % 2}', qdma)]
            if h == 0:
                wts += [('ve', a_ve), ('sc', a_sc), ('ve', kb_done)]
            else:
                wts += [('sc', exp31_prev)]
            pe_q = P.g('tensor', qms, waits=wts, inc='pe')
            pe_q_hist[h] = pe_q

            qev = P.g('vector', [
                (lambda half, h=h: (lambda e: e.tensor_scalar(
                    q_sb[h % 2][:, half * 512:(half + 1) * 512],
                    pq[0][:, half * 512:(half + 1) * 512],
                    bq_sb[:, h:h + 1], SCALE, op0=ALU.add, op1=ALU.mult)))(half)
                for half in range(2)
            ], waits=[('pe', pe_q), ('dCon', d_consts)], inc='ve')

            qsq = P.g('scalar', [
                (lambda half, h=h: (lambda e: e.activation(
                    out=csq[:, half * 512:(half + 1) * 512],
                    in_=q_sb[h % 2][:, half * 512:(half + 1) * 512], func=AF.Square)))(half)
                for half in range(2)
            ], waits=[('ve', qev)] + ([('pe', kb_pe_done)] if h == 0 else []), inc='sc')
            pe_qs = P.g('tensor', [
                lambda e: e.matmul(pb[4][0:1, 0:512], ocb_sb, csq[:, 0:512],
                                   start=True, stop=True, skip_group_check=True),
                lambda e: e.matmul(pb[5][0:1, 0:512], ocb_sb, csq[:, 512:1024],
                                   start=True, stop=True, skip_group_check=True),
            ], waits=[('sc', qsq)] + ([('ve', vrB_prev)] if vrB_prev else []), inc='pe')
            vq = P.g('vector', [
                lambda e: e.reduce_max(cms[0:1, 4:5], pb[4][0:1, 0:512], axis=AXX),
            ], waits=[('pe', pe_qs)], inc='ve')
            vq = P.g('vector', [
                lambda e: e.reduce_max(cms[0:1, 5:6], pb[5][0:1, 0:512], axis=AXX),
            ], waits=[('ve', vq)], inc='ve')
            vq = P.g('vector', [
                lambda e: e.tensor_max(cms[0:1, 4:5], cms[0:1, 4:5], cms[0:1, 5:6]),
            ], waits=[('ve', vq)], inc='ve')
            vq = P.g('vector', [
                (lambda kv=kv: (lambda e: e.tensor_mul(cms[0:1, 6:7], cms[0:1, 4:5],
                                                       cms[0:1, kv:kv + 1])))()
            ], waits=[('ve', vq)] + ([('ve', kb_done)] if h == 0 else []), inc='ve')
            scb = P.g('scalar', [
                lambda e: e.activation(out=cms[0:1, 7:8], in_=cms[0:1, 6:7], func=AF.Ln),
            ], waits=[('ve', vq)], inc='sc')
            scb = P.g('scalar', [
                lambda e: e.activation(out=cms[0:1, 7:8], in_=cms[0:1, 7:8], func=AF.Exp, scale=0.5),
            ], waits=[('sc', scb)], inc='sc')
            vnb = P.g('vector', [
                lambda e: e.tensor_scalar_mul(cms[0:1, 8:9], cms[0:1, 7:8], -1.0),
            ], waits=[('sc', scb)], inc='ve')
            penb = P.g('tensor', [
                lambda e: e.matmul(pb[4][:, 0:1], orf_sb, cms[0:1, 8:9],
                                   start=True, stop=True, skip_group_check=True),
            ], waits=[('ve', vnb)], inc='pe')
            vnbc = P.g('vector', [
                lambda e: e.tensor_copy(sml[:, 0:1], pb[4][:, 0:1]),
            ], waits=[('pe', penb), ('sc', exp31_prev)], inc='ve')

            exp_s, lav_s, scg_s = {}, {}, {}

            def emit_sc(sx):
                rot = sx % 2
                pi = sx % 4
                bA = pb[2] if rot == 0 else pb[0]
                bB = (pb[3] if rot == 0 else pb[1]) if sx < 16 else pb[sx % 4]
                ksl = slice(kv * S + sx * 128, kv * S + (sx + 1) * 128)
                scms = []
                if sx < 16:
                    scms.append((lambda ksl=ksl, h=h, bA=bA: (lambda e: e.matmul(
                        bA[:, 0:512], K_sb[:, ksl], q_sb[h % 2][:, 0:512],
                        start=True, stop=False, skip_group_check=True)))())
                    scms.append((lambda sx=sx, bA=bA: (lambda e: e.matmul(
                        bA[:, 0:512], nbi_sb, mkall[:, sx * 512:(sx + 1) * 512],
                        start=False, stop=True, skip_group_check=True)))())
                    scms.append((lambda ksl=ksl, h=h, bB=bB: (lambda e: e.matmul(
                        bB[:, 0:512], K_sb[:, ksl], q_sb[h % 2][:, 512:1024],
                        start=True, stop=True, skip_group_check=True)))())
                else:
                    scms.append((lambda ksl=ksl, h=h, bB=bB: (lambda e: e.matmul(
                        bB[:, 0:512], K_sb[:, ksl], q_sb[h % 2][:, 512:1024],
                        start=True, stop=False, skip_group_check=True)))())
                    scms.append((lambda sx=sx, bB=bB: (lambda e: e.matmul(
                        bB[:, 0:512], nbi_sb, mkall[:, sx * 512:(sx + 1) * 512],
                        start=False, stop=True, skip_group_check=True)))())
                wts = []
                if h == 0 and sx == 0:
                    wts.append(('dM0', d_mk))
                if sx <= 1:
                    wts += [('ve', qev), ('ve', vnbc)]
                elif sx < 16:
                    wts.append(('sc', exp_s[sx - 2]))
                elif sx < 20:
                    wts.append(('sc', exp_s[15]))
                else:
                    wts.append(('sc', exp_s[sx - 4]))
                scg_s[sx] = P.g('tensor', scms, waits=wts, inc='pe')

                exps = []
                if sx < 16:
                    exps.append((lambda rot=rot, pi=pi: (lambda e: e.activation(
                        out=p_sb[pi][:, 0:1024], in_=pq[1 - rot][:, 0:1024], func=AF.Exp,
                        bias=sml[:, 0:1])))())
                else:
                    exps.append((lambda pi=pi, bB=bB: (lambda e: e.activation(
                        out=p_sb[pi][:, 512:1024], in_=bB[:, 0:512], func=AF.Exp,
                        bias=sml[:, 0:1])))())
                pv = lav_s.get(sx - 4) or lav_prev4.get(sx - 4 + 32)
                exp_s[sx] = P.g('scalar', exps,
                                waits=[('pe', scg_s[sx]), ('ve', vnbc)] + ([('pe', pv)] if pv else []),
                                inc='sc')

            def emit_lav(sx):
                pi = sx % 4
                lav = []
                if sx < 16:
                    lav.append((lambda pi=pi, sx=sx: (lambda e: e.matmul(
                        pb[4][0:1, 0:512], ocb_sb, p_sb[pi][:, 0:512],
                        start=(sx == 0), stop=(sx == 15), skip_group_check=True)))())
                lav.append((lambda pi=pi, sx=sx: (lambda e: e.matmul(
                    pb[5][0:1, 0:512], ocb_sb, p_sb[pi][:, 512:1024],
                    start=(sx == 0), stop=(sx == 31), skip_group_check=True)))())
                vsl = slice(sx * 512 + kv * 128, sx * 512 + (kv + 1) * 128)
                if sx < 16:
                    lav.append((lambda pi=pi, vsl=vsl, sx=sx: (lambda e: e.matmul(
                        pb[6][:, 0:512], V_sb[:, vsl], p_sb[pi][:, 0:512],
                        start=(sx == 0), stop=(sx == 15), skip_group_check=True)))())
                lav.append((lambda pi=pi, vsl=vsl, sx=sx: (lambda e: e.matmul(
                    pb[7][:, 0:512], V_sb[:, vsl], p_sb[pi][:, 512:1024],
                    start=(sx == 0), stop=(sx == 31), skip_group_check=True)))())
                wts = [('sc', exp_s[sx])]
                if sx == 0:
                    wts.append(('ve', evac_prev))
                lav_s[sx] = P.g('tensor', lav, waits=wts, inc='pe')

            for sx in range(32):
                emit_sc(sx)
                if sx >= 2:
                    emit_lav(sx - 2)
            emit_lav(30)
            emit_lav(31)

            vr = P.g('vector', [
                lambda e: e.reciprocal(recA, pb[4][0:1, 0:512]),
            ], waits=[('pe', lav_s[31])], inc='ve')
            vr = P.g('vector', [
                lambda e: e.reciprocal(recB, pb[5][0:1, 0:512]),
            ], waits=[('ve', vr)], inc='ve')
            vrB_prev = vr
            perb = P.g('tensor', [
                lambda e: e.matmul(pb[2][:, 0:512], orf_sb, recA,
                                   start=True, stop=True, skip_group_check=True),
                lambda e: e.matmul(pb[3][:, 0:512], orf_sb, recB,
                                   start=True, stop=True, skip_group_check=True),
            ], waits=[('ve', vr), ('sc', exp_s[31])], inc='pe')
            vrb = P.g('vector', [
                lambda e: e.tensor_copy(rl_sb[:, 0:512], pb[2][:, 0:512]),
                lambda e: e.tensor_copy(rl_sb[:, 512:1024], pb[3][:, 0:512]),
            ], waits=[('pe', perb)], inc='ve')
            evac_prev = P.g('vector', [
                (lambda half, h=h: (lambda e: e.tensor_mul(
                    acat[:, h * TOWN + half * 512: h * TOWN + (half + 1) * 512],
                    pb[6 + half][:, 0:512], rl_sb[:, half * 512:(half + 1) * 512])))(half)
                for half in range(2)
            ], waits=[('ve', vrb)], inc='ve')
            exp31_prev = exp_s[31]
            scg31_prev = scg_s[31]
            lav_prev4 = {k: lav_s[k] for k in (28, 29, 30, 31)}

        c_heads_pe = P.cnt['pe']

        oc_ve = [None] * NCH
        oc_st = [None] * NCH
        for oc in range(NCH):
            if oc < 2:
                P.wait('gpsimd', 'pe', c_heads_pe)
            else:
                P.wait('gpsimd', 've', oc_ve[oc - 2])
            wodma = P.dma('gpsimd', wo_sb[oc % 2], wo_h[oc], sem=f'dO{oc % 2}')
            if oc >= 2:
                P.wait('sync', 've', oc_ve[oc - 2])
            else:
                P.wait('sync', 'pe', c_heads_pe)
            xocdma = P.dma('sync', xoc_sb[oc % 2], xo_oc[oc], sem=f'dXoc{oc % 2}')

            oms = []
            for half in range(2):
                for hc2 in range(NCH):
                    def mk_o(half=half, hc2=hc2, oc=oc):
                        return lambda e: e.matmul(
                            pb[2 * (oc % 2) + half][:, 0:512],
                            wo_sb[oc % 2][:, hc2 * 128:(hc2 + 1) * 128],
                            acat[:, hc2 * TOWN + half * 512: hc2 * TOWN + (half + 1) * 512],
                            start=(hc2 == 0), stop=(hc2 == NCH - 1), skip_group_check=True)
                    oms.append(mk_o())
            wts = [(f'dO{oc % 2}', wodma), ('ve', evac_prev)]
            if oc >= 2:
                wts.append(('ve', oc_ve[oc - 2]))
            pe_oc = P.g('tensor', oms, waits=wts, inc='pe')
            oc_ve[oc] = P.g('vector', [
                (lambda half, oc=oc: (lambda e: e.scalar_tensor_tensor(
                    ev_sb[oc % 2][:, half * 512:(half + 1) * 512],
                    pb[2 * (oc % 2) + half][:, 0:512], bo_sb[:, oc:oc + 1],
                    xoc_sb[oc % 2][:, half * 512:(half + 1) * 512],
                    op0=ALU.add, op1=ALU.add)))(half)
                for half in range(2)
            ], waits=[('pe', pe_oc), (f'dXoc{oc % 2}', xocdma),
                      (f'dSt{oc % 2}', oc_st[oc - 2] if oc >= 2 else 0)], inc='ve')
            P.wait('sync', 've', oc_ve[oc])
            for tt in range(2):
                v = P.dma('sync', xres_d[tt, :, oc, :], ev_sb[oc % 2][:, tt * TD:(tt + 1) * TD],
                          sem=f'dSt{oc % 2}')
            oc_st[oc] = v
        xres_st0, xres_st1 = P.cnt['dSt0'], P.cnt['dSt1']

        DM = slice(0, TD)
        DR = slice(TD, 2 * TD)
        D3 = slice(2 * TD, 3 * TD)
        prev_dn_ve = None
        prev_fl_pe = None
        prev_ap_ve = None
        dse_v = None
        last_st = {0: None, 1: None}
        for tt in range(2):
            P.wait('sync', 'dSt0', xres_st0)
            P.wait('sync', 'dSt1', xres_st1)
            if prev_dn_ve:
                P.wait('sync', 've', prev_dn_ve)
            xr_dma = P.dma('sync', xrt.rearrange('p (c n) -> p c n', c=NCH), xres_d[tt],
                           sem='dXr')
            P.wait('gpsimd', 'dSt0', xres_st0)
            P.wait('gpsimd', 'dSt1', xres_st1)
            if tt == 0:
                P.wait('gpsimd', 'pe', c_heads_pe)
            else:
                P.wait('gpsimd', 'pe', prev_down_pe)
            a1st = x2s[:, 8 * NCH * TD // TD * 0 + 16384:20480]
            a3st = x2s[:, 20480:24576]
            a1_dma = P.dma('gpsimd', a1st, a1_h, sem='dF0')
            a3_dma = P.dma('gpsimd', a3st, a3_h, sem='dF1')
            b2_dma = P.dma('gpsimd', b2_sb, b2_h, sem='dB2')

            wcast = [('dXr', xr_dma)]
            if prev_fl_pe:
                wcast.append(('pe', prev_fl_pe))
            if prev_dn_ve:
                wcast.append(('ve', prev_dn_ve))
            vcast = P.g('vector', [
                lambda e: e.tensor_copy(xb2a, xrt),
            ], waits=wcast, inc='ve')

            sqd_g = P.g('scalar', [
                lambda e: e.activation(out=sqda, in_=xb2a, func=AF.Square),
            ], waits=[('ve', vcast)], inc='sc')
            wts = [('ve', vcast)]
            if prev_fl_pe:
                wts.append(('pe', prev_fl_pe))
            dsum = P.g('tensor', [
                (lambda hc: (lambda e: e.matmul(pb[4][0:1, 0:TD], ocb_sb,
                                                xb2a[:, hc * TD:(hc + 1) * TD],
                                                start=(hc == 0), stop=(hc == NCH - 1),
                                                skip_group_check=True)))(hc)
                for hc in range(NCH)
            ], waits=wts, inc='pe')
            dsumsq = P.g('tensor', [
                (lambda hc: (lambda e: e.matmul(pb[5][0:1, 0:TD], ocb_sb,
                                                sqda[:, hc * TD:(hc + 1) * TD],
                                                start=(hc == 0), stop=(hc == NCH - 1),
                                                skip_group_check=True)))(hc)
                for hc in range(NCH)
            ], waits=[('sc', sqd_g)], inc='pe')
            v1 = P.g('vector', [
                lambda e: e.tensor_scalar_mul(rowsD[0:1, DM], pb[4][0:1, 0:TD], 1.0 / HID),
                lambda e: e.tensor_scalar_mul(rowsD[0:1, DR], pb[5][0:1, 0:TD], 1.0 / HID),
            ], waits=[('pe', dsumsq)], inc='ve')
            v2 = P.g('vector', [
                lambda e: e.tensor_mul(rowsD[0:1, D3], rowsD[0:1, DM], rowsD[0:1, DM]),
            ], waits=[('ve', v1)], inc='ve')
            dpre = P.g('vector', [
                lambda e: e.scalar_tensor_tensor(rowsD[0:1, DR], rowsD[0:1, DR], EPS,
                                                 rowsD[0:1, D3], op0=ALU.add, op1=ALU.subtract),
            ], waits=[('ve', v2)], inc='ve')
            dlnx = P.g('scalar', [
                lambda e: e.activation(out=rowsD[0:1, DR], in_=rowsD[0:1, DR], func=AF.Ln),
            ], waits=[('ve', dpre)], inc='sc')
            dlnx = P.g('scalar', [
                lambda e: e.activation(out=rowsD[0:1, DR], in_=rowsD[0:1, DR], func=AF.Exp, scale=-0.5),
            ], waits=[('sc', dlnx)], inc='sc')
            dpost = P.g('vector', [
                lambda e: e.scalar_tensor_tensor(rowsD[0:1, D3], rowsD[0:1, DM], -1.0,
                                                 rowsD[0:1, DR], op0=ALU.mult, op1=ALU.mult),
            ], waits=[('sc', dlnx)], inc='ve')
            dbc = P.g('tensor', [
                lambda e: e.matmul(pb[6][:, 0:TD], orf_sb, rowsD[0:1, DR],
                                   start=True, stop=True, skip_group_check=True),
                lambda e: e.matmul(pb[7][:, 0:TD], orf_sb, rowsD[0:1, D3],
                                   start=True, stop=True, skip_group_check=True),
            ], waits=[('ve', dpost)], inc='pe')
            drbb = P.g('vector', [
                lambda e: e.tensor_copy(rbb2[:, 0:TD], pb[6][:, 0:TD]),
                lambda e: e.tensor_copy(rbb2[:, TD:2 * TD], pb[7][:, 0:TD]),
            ], waits=[('pe', dbc)], inc='ve')
            xln2v = xln2.rearrange('p (c n) -> p c n', c=NCH)
            xb2av = xb2a.rearrange('p (c n) -> p c n', c=NCH)

            def bc_rbb2(lo):
                return rbb2[:, lo:lo + TD].rearrange(
                    'p (o n) -> p o n', o=1).to_broadcast([128, NCH, TD])
            dap1 = P.g('vector', [
                lambda e: e.tensor_mul(xln2v, xb2av, bc_rbb2(0)),
            ], waits=[('ve', drbb)], inc='ve')
            d_ap = P.g('vector', [
                lambda e: e.tensor_add(xln2v, xln2v, bc_rbb2(TD)),
            ], waits=[('ve', dap1)], inc='ve')
            prev_ap_ve = d_ap
            dnm = P.g('vector', [
                lambda e: e.tensor_scalar_mul(rowsD[0:1, D3], rowsD[0:1, DM], -1.0),
            ], waits=[('ve', drbb)], inc='ve')

            a13_pe = [None, None]
            for which in range(2):
                ams = []
                for eh in range(2):
                    for hc in range(NCH):
                        def mk_a(which=which, eh=eh, hc=hc):
                            bank = pb[4 + eh] if which == 0 else pb[6 + eh]
                            buf = a1st if which == 0 else a3st
                            return lambda e, bank=bank, buf=buf: e.matmul(
                                bank[:, 0:TD],
                                buf[:, hc * 256 + eh * 128: hc * 256 + (eh + 1) * 128],
                                xln2[:, hc * TD:(hc + 1) * TD],
                                start=(hc == 0), stop=(hc == NCH - 1), skip_group_check=True)
                        ams.append(mk_a())
                a13_pe[which] = P.g('tensor', ams,
                                    waits=[('ve', d_ap), ('dF0', a1_dma), ('dF1', a3_dma)], inc='pe')


            if tt == 0:
                dse = P.g('tensor', [
                    (lambda hc: (lambda e: e.matmul(pb[3][0:1, 0:8], ocf_sb,
                                                    wgt_sb[:, hc * 8:(hc + 1) * 8],
                                                    start=(hc == 0), stop=(hc == NCH - 1),
                                                    skip_group_check=True)))(hc)
                    for hc in range(NCH)
                ], waits=[('dCon', d_consts), ('pe', c_heads_pe)], inc='pe')
                dse_v = P.g('vector', [
                    lambda e: e.tensor_copy(serow, pb[3][0:1, 0:8]),
                ], waits=[('pe', dse)], inc='ve')

            ts_ve = None
            for ts in range(4):
                dms = []
                for hc in range(NCH):
                    def mk_d(hc=hc, ts=ts):
                        return lambda e: e.matmul(
                            pb[2][:, 0:8],
                            xrt[:, hc * TD + ts * 128: hc * TD + ts * 128 + 128],
                            wgt_sb[:, hc * 8:(hc + 1) * 8],
                            start=(hc == 0), stop=False, skip_group_check=True)
                    dms.append(mk_d())
                dms.append((lambda ts=ts: (lambda e: e.matmul(
                    pb[2][:, 0:8], rowsD[0:1, 2 * TD + ts * 128: 2 * TD + ts * 128 + 128],
                    serow, start=False, stop=True, skip_group_check=True)))())
                wts = [('dXr', xr_dma), ('ve', dnm), ('ve', dse_v)]
                if ts > 0:
                    wts.append(('ve', ts_ve))
                ddot = P.g('tensor', dms, waits=wts, inc='pe')
                drc = P.g('tensor', [
                    (lambda ts=ts: (lambda e: e.matmul(
                        pb[3][:, 8:9], rowsD[0:1, TD + ts * 128: TD + ts * 128 + 128],
                        one_sb, start=True, stop=True, skip_group_check=True)))()
                ], waits=[('sc', dlnx), ('dCon', d_consts)] + ([('ve', ts_ve)] if ts else []), inc='pe')
                gv = P.g('vector', [
                    lambda e: e.tensor_copy(sml[:, 1:2], pb[3][:, 8:9]),
                ], waits=[('pe', drc)], inc='ve')
                gv = P.g('vector', [
                    lambda e: e.tensor_scalar(gat[:, 0:8], pb[2][:, 0:8], sml[:, 1:2], None, op0=ALU.mult),
                ], waits=[('ve', gv), ('pe', ddot)], inc='ve')
                gv = P.g('vector', [
                    lambda e: e.max(out=gat[:, 32:40], in_=gat[:, 0:8]),
                ], waits=[('ve', gv)], inc='ve')
                gv = P.g('vector', [
                    lambda e: e.tensor_scalar(gat[:, 8:16], gat[:, 0:8], gat[:, 32:33], None, op0=ALU.is_equal),
                ], waits=[('ve', gv)], inc='ve')
                gv = P.g('vector', [
                    lambda e: e.scalar_tensor_tensor(gat[:, 16:24], gat[:, 8:16], -1e9,
                                                     gat[:, 0:8], op0=ALU.mult, op1=ALU.add),
                ], waits=[('ve', gv)], inc='ve')
                gv = P.g('vector', [
                    lambda e: e.tensor_scalar(gat[:, 24:32], gat[:, 16:24], gat[:, 33:34], None, op0=ALU.is_equal),
                    lambda e: e.tensor_sub(sml[:, 4:5], gat[:, 32:33], gat[:, 33:34]),
                ], waits=[('ve', gv)], inc='ve')
                gsc = P.g('scalar', [
                    lambda e: e.activation(out=sml[:, 5:6], in_=sml[:, 4:5], func=AF.Exp, scale=-1.0),
                ], waits=[('ve', gv)], inc='sc')
                gv = P.g('vector', [
                    lambda e: e.tensor_scalar_add(sml[:, 5:6], sml[:, 5:6], 1.0),
                ], waits=[('sc', gsc)], inc='ve')
                gv = P.g('vector', [
                    lambda e: e.reciprocal(sml[:, 2:3], sml[:, 5:6]),
                ], waits=[('ve', gv)], inc='ve')
                gv = P.g('vector', [
                    lambda e: e.tensor_scalar(sml[:, 3:4], sml[:, 2:3], -1.0, 1.0, op0=ALU.mult, op1=ALU.add),
                ], waits=[('ve', gv)], inc='ve')
                gv = P.g('vector', [
                    (lambda ts=ts: (lambda e: e.tensor_copy(rw0c[:, ts:ts + 1], sml[:, 2:3])))(),
                    lambda e: e.tensor_scalar(gat[:, 40:48], gat[:, 8:16], sml[:, 2:3], None, op0=ALU.mult),
                    lambda e: e.tensor_scalar(gat[:, 48:56], gat[:, 24:32], sml[:, 3:4], None, op0=ALU.mult),
                ], waits=[('ve', gv)], inc='ve')
                dtp = P.g('tensor', [
                    (lambda col, mi: (lambda e: e.matmul(
                        pb[3][0:8, mi * 128:(mi + 1) * 128], gat[:, col:col + 8], idf_sb,
                        start=True, stop=True, skip_group_check=True)))(col, mi)
                    for mi, col in enumerate((8, 24, 40, 48))
                ], waits=[('ve', gv)], inc='pe')
                ts_ve = P.g('vector', [
                    (lambda mi, ts=ts: (lambda e: e.tensor_copy(
                        gat2[mi][0:8, ts * 128:(ts + 1) * 128],
                        pb[3][0:8, mi * 128:(mi + 1) * 128])))(mi)
                    for mi in range(4)
                ], waits=[('pe', dtp)], inc='ve')

            prw = P.g('tensor', [
                (lambda ts: (lambda e: e.matmul(
                    pb[2][0:1, ts * 128:(ts + 1) * 128], rw0c[:, ts:ts + 1], idf_sb,
                    start=True, stop=True, skip_group_check=True)))(ts)
                for ts in range(4)
            ], waits=[('ve', ts_ve)], inc='pe')
            vrw = P.g('vector', [
                lambda e: e.tensor_copy(krow, pb[2][0:1, 0:512]),
            ], waits=[('pe', prw)], inc='ve')
            prwb = P.g('tensor', [
                lambda e: e.matmul(pb[2][:, 0:TD], orf_sb, krow,
                                   start=True, stop=True, skip_group_check=True),
            ], waits=[('ve', vrw)], inc='pe')
            vrwb = P.g('vector', [
                lambda e: e.tensor_copy(rwb0, pb[2][:, 0:TD]),
            ], waits=[('pe', prwb)], inc='ve')

            dexp_ve = None
            for mi, dstbuf in ((0, mer[0]), (1, mer[1]), (2, wmer[0]), (3, wmer[1])):
                for eh in range(2):
                    wts = [('ve', ts_ve)]
                    if dexp_ve:
                        wts.append(('ve', dexp_ve))
                    pee = P.g('tensor', [
                        (lambda mi=mi, eh=eh: (lambda e: e.matmul(
                            pb[3][:, 0:TD], exps_sb[:, eh * 128:(eh + 1) * 128],
                            gat2[mi][0:8, 0:TD],
                            start=True, stop=True, skip_group_check=True)))()
                    ], waits=wts, inc='pe')
                    dexp_ve = P.g('vector', [
                        (lambda eh=eh, dstbuf=dstbuf: (lambda e: e.tensor_copy(
                            dstbuf[:, eh * TD:(eh + 1) * TD], pb[3][:, 0:TD])))()
                    ], waits=[('pe', pee)], inc='ve')

            dc_ve = P.g('vector', [
                (lambda w, k, eh: (lambda e: e.tensor_mul(
                    (c1 if w == 0 else c3)[k][:, eh * TD:(eh + 1) * TD],
                    (pb[4 + eh] if w == 0 else pb[6 + eh])[:, 0:TD],
                    mer[k][:, eh * TD:(eh + 1) * TD])))(w, k, eh)
                for w in range(2) for k in range(2) for eh in range(2)
            ], waits=[('pe', a13_pe[1]), ('ve', dexp_ve)], inc='ve')
            dc_ve = P.g('vector', [
                (lambda eh: (lambda e: e.tensor_scalar_mul(
                    nc1[:, eh * TD:(eh + 1) * TD], c1[0][:, eh * TD:(eh + 1) * TD], -1.0)))(eh)
                for eh in range(2)
            ] + [
                (lambda eh: (lambda e: e.tensor_scalar_mul(
                    nc3[:, eh * TD:(eh + 1) * TD], c3[0][:, eh * TD:(eh + 1) * TD], -1.0)))(eh)
                for eh in range(2)
            ], waits=[('ve', dc_ve), ('ve', vrwb)], inc='ve')

            silu1_s, x2ve, a2pe, pair_pe = {}, {}, {}, {}
            dma_b13, dma_a2, w3d_s = {}, {}, {}
            hx1, hx3 = {}, {}
            vx20_s, vx21_s, a20_s, a21_s, pex3b_s = {}, {}, {}, {}, {}

            def emit_heads(fc):
                j2 = fc // 2
                u = fc % 2
                q4 = fc // 4
                if fc >= 2:
                    P.wait('gpsimd', 'pe', hx3[fc - 2])
                else:
                    P.wait('gpsimd', 've', dc_ve)
                P.dma('gpsimd', w13[fc % 2][:, 0:NCH * 128], w1_h[fc], sem=f'dF{fc % 2}')
                w3d_s[fc] = P.dma('gpsimd', w13[fc % 2][:, NCH * 128:2 * NCH * 128],
                                  w3_h[fc], sem=f'dF{fc % 2}')
                if u == 0:
                    if fc >= 3:
                        P.wait('gpsimd', 'pe', pex3b_s[fc - 3])
                    P.dma('gpsimd', b13[j2 % 2][:, 0:512], b1_h[j2], sem=f'dB{j2 % 2}')
                    dma_b13[j2] = P.dma('gpsimd', b13[j2 % 2][:, 512:1024], b3_h[j2],
                                        sem=f'dB{j2 % 2}')
                if fc % 4 == 0:
                    if fc >= 5:
                        P.wait('gpsimd', 'pe', a2pe[fc - 5])
                    dma_a2[q4] = P.dma('gpsimd', a2s[q4 % 2], a2_h[q4], sem=f'dA{q4 % 2}')

                bx1 = pb[fc % 2]
                bx3 = pb[2 + fc % 2]
                b13buf = b13[j2 % 2]

                def lw13(hc, w, fc=fc):
                    return w13[fc % 2][:, w * NCH * 128 + hc * 128: w * NCH * 128 + (hc + 1) * 128]

                def lb13(w, eh, u=u, b13buf=b13buf):
                    off = w * 512 + eh * 256 + u * 128
                    return b13buf[:, off:off + 128]

                x1ms = [(lambda hc, bx1=bx1, w=lw13: (lambda e: e.matmul(
                    bx1[:, 0:TD], w(hc, 0), xln2[:, hc * TD:(hc + 1) * TD],
                    start=(hc == 0), stop=False, skip_group_check=True)))(hc) for hc in range(NCH)]
                x1ms += [(lambda eh, bx1=bx1, b=lb13: (lambda e: e.matmul(
                    bx1[:, 0:TD], b(0, eh), c1[0][:, eh * TD:(eh + 1) * TD],
                    start=False, stop=(eh == 1), skip_group_check=True)))(eh) for eh in range(2)]
                wts = [(f'dF{fc % 2}', w3d_s[fc]), (f'dB{j2 % 2}', dma_b13[j2]), ('ve', dc_ve)]
                if fc >= 2:
                    wts.append(('sc', silu1_s[fc - 2]))
                hx1[fc] = P.g('tensor', x1ms, waits=wts, inc='pe')

                x3ms = [(lambda hc, bx3=bx3, w=lw13: (lambda e: e.matmul(
                    bx3[:, 0:TD], w(hc, 1), xln2[:, hc * TD:(hc + 1) * TD],
                    start=(hc == 0), stop=False, skip_group_check=True)))(hc) for hc in range(NCH)]
                x3ms += [(lambda eh, bx3=bx3, b=lb13: (lambda e: e.matmul(
                    bx3[:, 0:TD], b(1, eh), c3[0][:, eh * TD:(eh + 1) * TD],
                    start=False, stop=(eh == 1), skip_group_check=True)))(eh) for eh in range(2)]
                wts = []
                if fc >= 2:
                    wts.append(('ve', vx21_s[fc - 2]))
                hx3[fc] = P.g('tensor', x3ms, waits=wts, inc='pe')

            def emit_tails(fc):
                j2 = fc // 2
                u = fc % 2
                q4 = fc // 4
                par = fc % 2
                bx1 = pb[fc % 2]
                bx3 = pb[2 + fc % 2]
                b13buf = b13[j2 % 2]
                a2buf = a2s[q4 % 2]

                def lb13(w, eh, u=u, b13buf=b13buf):
                    off = w * 512 + eh * 256 + u * 128
                    return b13buf[:, off:off + 128]

                wts = [('pe', hx1[fc])]
                if fc >= 2:
                    wts.append(('ve', vx20_s[fc - 2]))
                s0 = P.g('scalar', [(lambda bx1=bx1, par=par: (lambda e: e.activation(
                    out=s_scr[0][par], in_=bx1[:, 0:TD], func=AF.Silu)))()],
                    waits=wts, inc='sc')
                x1b = [(lambda eh, bx1=bx1, b=lb13: (lambda e: e.matmul(
                    bx1[:, 0:TD], b(0, eh), nc1[:, eh * TD:(eh + 1) * TD],
                    start=False, stop=False, skip_group_check=True)))(eh) for eh in range(2)]
                x1b += [(lambda eh, bx1=bx1, b=lb13: (lambda e: e.matmul(
                    bx1[:, 0:TD], b(0, eh), c1[1][:, eh * TD:(eh + 1) * TD],
                    start=False, stop=(eh == 1), skip_group_check=True)))(eh) for eh in range(2)]
                pex1b = P.g('tensor', x1b, waits=[('sc', s0)], inc='pe')
                wts = [('pe', pex1b)]
                if fc >= 2:
                    wts.append(('ve', vx21_s[fc - 2]))
                s1 = P.g('scalar', [(lambda bx1=bx1, par=par: (lambda e: e.activation(
                    out=s_scr[1][par], in_=bx1[:, 0:TD], func=AF.Silu)))()],
                    waits=wts, inc='sc')
                silu1_s[fc] = s1
                wts = [('sc', s0), ('pe', hx3[fc])]
                if fc >= 2:
                    wts.append(('pe', a20_s[fc - 2]))
                vx20_s[fc] = P.g('vector', [(lambda bx3=bx3, par=par: (lambda e: e.tensor_mul(
                    x2_scr[0][par], s_scr[0][par], bx3[:, 0:TD])))()],
                    waits=wts, inc='ve')
                a20_s[fc] = P.g('tensor', [(lambda eh, fc=fc, a2buf=a2buf, par=par: (lambda e: e.matmul(
                    pb[4 + eh][:, 0:TD],
                    a2buf[:, (fc % 4) * 256 + eh * 128:(fc % 4) * 256 + (eh + 1) * 128],
                    x2_scr[0][par], start=(fc == 0), stop=(fc == NFC - 1),
                    skip_group_check=True)))(eh) for eh in range(2)],
                    waits=[('ve', vx20_s[fc]), (f'dA{q4 % 2}', dma_a2[q4])], inc='pe')
                x3b = [(lambda eh, bx3=bx3, b=lb13: (lambda e: e.matmul(
                    bx3[:, 0:TD], b(1, eh), nc3[:, eh * TD:(eh + 1) * TD],
                    start=False, stop=False, skip_group_check=True)))(eh) for eh in range(2)]
                x3b += [(lambda eh, bx3=bx3, b=lb13: (lambda e: e.matmul(
                    bx3[:, 0:TD], b(1, eh), c3[1][:, eh * TD:(eh + 1) * TD],
                    start=False, stop=(eh == 1), skip_group_check=True)))(eh) for eh in range(2)]
                pex3b = P.g('tensor', x3b, waits=[('ve', vx20_s[fc])], inc='pe')
                pex3b_s[fc] = pex3b
                wts = [('sc', s1), ('pe', pex3b)]
                if fc >= 2:
                    wts.append(('pe', a21_s[fc - 2]))
                vx21_s[fc] = P.g('vector', [(lambda bx3=bx3, par=par: (lambda e: e.tensor_mul(
                    x2_scr[1][par], s_scr[1][par], bx3[:, 0:TD])))()],
                    waits=wts, inc='ve')
                vd = P.g('vector', [(lambda par=par: (lambda e: e.tensor_sub(
                    d_scr[par], x2_scr[0][par], x2_scr[1][par])))()],
                    waits=[('ve', vx21_s[fc])], inc='ve')
                vd = P.g('vector', [(lambda par=par: (lambda e: e.tensor_mul(
                    d_scr[par], d_scr[par], rwb0)))()],
                    waits=[('ve', vd)], inc='ve')
                x2ve[fc] = P.g('vector', [(lambda fc=fc, par=par: (lambda e: e.tensor_add(
                    x2s[:, fc * TD:(fc + 1) * TD], d_scr[par], x2_scr[1][par])))()],
                    waits=[('ve', vd)], inc='ve')

            def emit_a21(fc):
                q4 = fc // 4
                par = fc % 2
                a2buf = a2s[q4 % 2]
                a21 = P.g('tensor', [(lambda eh, fc=fc, a2buf=a2buf, par=par: (lambda e: e.matmul(
                    pb[6 + eh][:, 0:TD],
                    a2buf[:, (fc % 4) * 256 + eh * 128:(fc % 4) * 256 + (eh + 1) * 128],
                    x2_scr[1][par], start=(fc == 0), stop=(fc == NFC - 1),
                    skip_group_check=True)))(eh) for eh in range(2)],
                    waits=[('ve', vx21_s[fc])], inc='pe')
                a21_s[fc] = a21
                a2pe[fc] = a21
                pair_pe[fc] = a21

            for fc in range(NFC):
                emit_heads(fc)
                if fc >= 1:
                    emit_tails(fc - 1)
                if fc >= 2:
                    emit_a21(fc - 2)
            emit_tails(NFC - 1)
            emit_a21(NFC - 2)
            emit_a21(NFC - 1)
            prev_fl_pe = a2pe[NFC - 1]

            c2tmp = c3[0]
            v = P.g('vector', [
                (lambda eh: (lambda e: e.tensor_mul(
                    c1[0][:, eh * TD:(eh + 1) * TD], pb[4 + eh][:, 0:TD],
                    wmer[0][:, eh * TD:(eh + 1) * TD])))(eh)
                for eh in range(2)
            ] + [
                (lambda eh: (lambda e: e.tensor_mul(
                    c2tmp[:, eh * TD:(eh + 1) * TD], pb[6 + eh][:, 0:TD],
                    wmer[1][:, eh * TD:(eh + 1) * TD])))(eh)
                for eh in range(2)
            ], waits=[('pe', prev_fl_pe), ('ve', x2ve[NFC - 1])], inc='ve')
            c2s_ve = P.g('vector', [
                (lambda eh: (lambda e: e.tensor_add(
                    c1[0][:, eh * TD:(eh + 1) * TD], c1[0][:, eh * TD:(eh + 1) * TD],
                    c2tmp[:, eh * TD:(eh + 1) * TD])))(eh)
                for eh in range(2)
            ], waits=[('ve', v)], inc='ve')

            dn_ve = [None] * NCH
            dn_st = [None] * NCH
            qgrp_pe = {}
            for hc in range(NCH):
                qpe = None
                for q in range(4):
                    idx = hc * 4 + q
                    if idx >= 3:
                        P.wait('gpsimd', 'pe', qgrp_pe[idx - 3])
                    elif tt == 0 and idx == 0:
                        P.wait('gpsimd', 've', c2s_ve)
                    elif tt == 1 and idx == 0:
                        P.wait('gpsimd', 've', c2s_ve)
                    w2dma = P.dma('gpsimd', w2s[idx % 3], w2_h[hc, q], sem=('dW0', 'dW1', 'dW2')[idx % 3])
                    dms = []
                    for jj in range(14):
                        def mk_w2(q=q, jj=jj, hc=hc, idx=idx):
                            fc2 = 14 * q + jj
                            return lambda e: e.matmul(
                                pb[hc % 2][:, 0:TD], w2s[idx % 3][:, jj * 128:(jj + 1) * 128],
                                x2s[:, fc2 * TD:(fc2 + 1) * TD],
                                start=(q == 0 and jj == 0), stop=False, skip_group_check=True)
                        dms.append(mk_w2())
                    wts = [(('dW0', 'dW1', 'dW2')[idx % 3], w2dma), ('ve', x2ve[NFC - 1])]
                    if q == 0 and hc >= 2:
                        wts.append(('ve', dn_ve[hc - 2]))
                    qpe = P.g('tensor', dms, waits=wts, inc='pe')
                    qgrp_pe[idx] = qpe
                l2 = [(lambda eh, hc=hc: (lambda e: e.matmul(
                    pb[hc % 2][:, 0:TD], b2_sb[:, eh * HID + hc * 128: eh * HID + (hc + 1) * 128],
                    c1[0][:, eh * TD:(eh + 1) * TD],
                    start=False, stop=(eh == 1), skip_group_check=True)))(eh) for eh in range(2)]
                pedn = P.g('tensor', l2, waits=[('ve', c2s_ve), ('dB2', b2_dma)], inc='pe')
                wts = [('pe', pedn), ('dXr', xr_dma),
                       (f'oS{hc % 2}', last_st[hc % 2])]
                dn_ve[hc] = P.g('vector', [
                    (lambda hc=hc: (lambda e: e.tensor_add(
                        dn[hc % 2], pb[hc % 2][:, 0:TD], xrt[:, hc * TD:(hc + 1) * TD])))()
                ], waits=wts, inc='ve')
                P.wait('sync', 've', dn_ve[hc])
                dn_st[hc] = P.dma('sync', out_d[hc, :, tt, :], dn[hc % 2], sem=f'oS{hc % 2}')
                last_st[hc % 2] = dn_st[hc]
            prev_dn_ve = dn_ve[NCH - 1]
            prev_down_pe = P.cnt['pe']

        n_out0, n_out1 = P.cnt['oS0'], P.cnt['oS1']

        @blk.sync
        def _(e):
            for f in P.streams['sync']:
                f(e)
            e.wait_ge(sems['oS0'], n_out0)
            e.wait_ge(sems['oS1'], n_out1)

        @blk.gpsimd
        def _(e):
            for f in P.streams['gpsimd']:
                f(e)

        @blk.tensor
        def _(e):
            for f in P.streams['tensor']:
                f(e)

        @blk.vector
        def _(e):
            for f in P.streams['vector']:
                f(e)

        @blk.scalar
        def _(e):
            for f in P.streams['scalar']:
                f(e)

    return nc


_CACHED = {}


def _prep(inputs):
    f32 = np.float32
    x = np.asarray(inputs['L0_attn_x'], f32)
    wq = np.asarray(inputs['L0_attn_wq'], f32)
    bq = np.asarray(inputs['L0_attn_bq'], f32)
    wk = np.asarray(inputs['L0_attn_wk'], f32)
    bk = np.asarray(inputs['L0_attn_bk'], f32)
    wv = np.asarray(inputs['L0_attn_wv'], f32)
    bv = np.asarray(inputs['L0_attn_bv'], f32)
    wo = np.asarray(inputs['L0_attn_wo'], f32)
    bo = np.asarray(inputs['L0_attn_bo'], f32)
    Wg = np.asarray(inputs['L0_ffn_Wg'], f32)
    W1 = np.asarray(inputs['L0_ffn_W1'], f32)
    W2 = np.asarray(inputs['L0_ffn_W2'], f32)
    W3 = np.asarray(inputs['L0_ffn_W3'], f32)
    A1 = np.asarray(inputs['L0_ffn_A1'], f32)
    B1 = np.asarray(inputs['L0_ffn_B1'], f32)
    A2 = np.asarray(inputs['L0_ffn_A2'], f32)
    B2 = np.asarray(inputs['L0_ffn_B2'], f32)
    A3 = np.asarray(inputs['L0_ffn_A3'], f32)
    B3 = np.asarray(inputs['L0_ffn_B3'], f32)

    C = np.ascontiguousarray
    sh = {}
    sh['wq_h'] = C(wq.reshape(NCH, 128, NH, 128).transpose(2, 1, 0, 3).reshape(NH, 128, NCH * 128))
    sh['wk_h'] = C(wk.reshape(NCH, 128, 512).transpose(1, 0, 2).reshape(128, NCH * 512))
    sh['wv_h'] = C(wv.reshape(NCH, 128, 512).transpose(1, 0, 2).reshape(128, NCH * 512))
    sh['wo_h'] = C(wo.reshape(NCH, 128, NCH, 128).transpose(2, 1, 0, 3).reshape(NCH, 128, NCH * 128))
    sh['bq_h'] = C(bq.reshape(NH, 128).T)
    sh['bk_h'] = C(bk.reshape(NKV, 128).T)
    sh['bo_h'] = C(bo.reshape(NCH, 128).T)
    sh['bv_h'] = C(np.broadcast_to(bv[None, :], (128, 512)))
    sh['wgt_h'] = C(Wg.T.reshape(NCH, 128, 8).transpose(1, 0, 2).reshape(128, NCH * 8))
    sh['w1_h'] = C(W1.T.reshape(NCH, 128, NFC, 128).transpose(2, 1, 0, 3).reshape(NFC, 128, NCH * 128))
    sh['w3_h'] = C(W3.T.reshape(NCH, 128, NFC, 128).transpose(2, 1, 0, 3).reshape(NFC, 128, NCH * 128))
    sh['w2_h'] = C(W2.T.reshape(4, 14, 128, NCH, 128).transpose(3, 0, 2, 1, 4).reshape(NCH, 4, 128, 14 * 128))
    A1pp = A1.reshape(256, HID)
    A3pp = A3.reshape(256, HID)
    sh['a1_h'] = C(A1pp.reshape(256, NCH, 128).transpose(2, 1, 0).reshape(128, NCH * 256))
    sh['a3_h'] = C(A3pp.reshape(256, NCH, 128).transpose(2, 1, 0).reshape(128, NCH * 256))
    B1r = B1.transpose(0, 2, 1).reshape(256, FF)
    B3r = B3.transpose(0, 2, 1).reshape(256, FF)
    sh['b1_h'] = C(B1r.reshape(2, 128, 28, 256).transpose(2, 1, 0, 3).reshape(28, 128, 2 * 256))
    sh['b3_h'] = C(B3r.reshape(2, 128, 28, 256).transpose(2, 1, 0, 3).reshape(28, 128, 2 * 256))
    A2pp = A2.reshape(256, FF)
    sh['a2_h'] = C(A2pp.reshape(256, 14, 4, 128).transpose(1, 3, 2, 0).reshape(14, 128, 4 * 256))
    B2r = B2.transpose(0, 2, 1).reshape(256, HID)
    sh['b2_h'] = C(B2r.reshape(2, 128, HID).transpose(1, 0, 2).reshape(128, 2 * HID))

    in_maps = []
    ki = np.arange(128)
    qi = np.arange(512)
    for core in range(8):
        b = core // 4
        c = core % 4
        m = dict(sh)
        xT = C(x[b].T)
        m['x_tiles'] = C(xT.reshape(NCH, 128, NTA, TA).transpose(2, 1, 0, 3).reshape(NTA, 128, NCH * TA))
        own = C(np.concatenate([xT[:, 512 * c:512 * (c + 1)],
                                xT[:, 512 * (7 - c):512 * (8 - c)]], axis=1))
        m['xo_tiles'] = C(own.reshape(NCH, 128, NTB, TA).transpose(2, 1, 0, 3).reshape(NTB, 128, NCH * TA))
        m['xo_oc'] = C(own.reshape(NCH, 128, TOWN))
        mkk = np.zeros((128, 32, 512), np.float32)
        for sx in range(32):
            qpos = 512 * c + qi if sx < 16 else 512 * (7 - c) + qi
            kpos = 128 * sx + ki
            mkk[:, sx, :] = (qpos[None, :] < kpos[:, None]).astype(np.float32)
        m['masks_h'] = mkk.reshape(128, 32 * 512).astype(ml_dtypes.bfloat16)
        in_maps.append(m)
    return in_maps


def kernel(**inputs):
    in_maps = _prep(inputs)
    if 'nc' not in _CACHED:
        _CACHED['nc'] = build_program()
    nc = _CACHED['nc']
    res = run_bass_kernel_spmd(nc, in_maps, core_ids=list(range(8)))
    out = np.empty((2, HID, S), np.float32)
    for core in range(8):
        b, c = core // 4, core % 4
        o = np.asarray(res.results[core]['out_d'], np.float32)
        out[b, :, 512 * c:512 * (c + 1)] = o[:, :, 0, :].reshape(HID, TD)
        out[b, :, 512 * (7 - c):512 * (8 - c)] = o[:, :, 1, :].reshape(HID, TD)
    return np.ascontiguousarray(out.transpose(0, 2, 1))
